# version 06
# baseline (speedup 1.0000x reference)
import contextlib
import numpy as np
import ml_dtypes
import concourse.bass as bass
import concourse.mybir as mybir
from concourse.bass_utils import run_bass_kernel_spmd

F32 = mybir.dt.float32
BF16 = mybir.dt.bfloat16
I8 = mybir.dt.int8
AF = mybir.ActivationFunctionType
ALU = mybir.AluOpType

T = 8192
D = 2048
DBG_B = 0
H = 16
HC = 4
R = 4
DK = 128
DV = 128
C = 32
BLK = 1024
NBLK = T // BLK
CPB = BLK // C
SCALE = DK ** -0.5
NORM_EPS = 1e-5
LN_EPS = 1e-5
TQ = T // 4


class Plan:

    def __init__(self):
        self.items = []
        self.cnt = {}
        self.seen = {}

    def val(self, sem):
        return self.cnt.get(sem, 0)

    def bar(self):
        return [(x, self.val(x)) for x in ("pe", "dv", "ac", "po", "ld", "st")]

    def emit(self, eng, fn, waits=(), inc=None, amt=1):
        fw = []
        seen = self.seen.setdefault(eng, {})
        for s, v in waits:
            if v > 0 and seen.get(s, 0) < v:
                fw.append((s, v))
                seen[s] = v
        if inc is not None:
            self.cnt[inc] = self.cnt.get(inc, 0) + amt
        self.items.append((eng, fn, tuple(fw), inc, amt))
        return (inc, self.cnt.get(inc, 0)) if inc is not None else None

    def body(self, eng_name, sems):
        items = self.items

        def body_fn(e):
            for en, fn, fw, inc, amt in items:
                if en != eng_name:
                    continue
                for s, v in fw:
                    e.wait_ge(sems[s], v)
                ins = fn(e)
                if inc is not None:
                    if ins is None:
                        ins = e.engine_nop()
                    ins.then_inc(sems[inc], amt)
        return body_fn


def _run_plan(nc, plan, sems):
    with nc.Block() as blk:
        blk.sync(plan.body("sync", sems))
        blk.gpsimd(plan.body("gpsimd", sems))
        blk.tensor(plan.body("tensor", sems))
        blk.vector(plan.body("vector", sems))
        blk.scalar(plan.body("scalar", sems))


def build_mixer():
    nc = bass.Bass("TRN2", target_bir_lowering=False)
    dt = nc.dram_tensor
    xT = dt("xT", [D, T], F32, kind="ExternalInput").ap()
    wqT = dt("wqT", [D, HC * DK], F32, kind="ExternalInput").ap()
    wkT = dt("wkT", [D, HC * R * DK], F32, kind="ExternalInput").ap()
    wvT = dt("wvT", [D, HC * R * DV], F32, kind="ExternalInput").ap()
    wbT = dt("wbT", [D, HC * R], F32, kind="ExternalInput").ap()
    wf1T = dt("wf1T", [D, DK], F32, kind="ExternalInput").ap()
    wg1T = dt("wg1T", [D, DV], F32, kind="ExternalInput").ap()
    wf2T = dt("wf2T", [DK, HC * DK], F32, kind="ExternalInput").ap()
    wg2T = dt("wg2T", [DV, HC * DV], F32, kind="ExternalInput").ap()
    convq = dt("convq", [HC * DK, 4], F32, kind="ExternalInput").ap()
    convk = dt("convk", [HC * R * DK, 4], F32, kind="ExternalInput").ap()
    convv = dt("convv", [HC * R * DV, 4], F32, kind="ExternalInput").ap()
    dtb = dt("dtb", [HC * DK, 1], F32, kind="ExternalInput").ap()
    alogr = dt("alogr", [1, HC], F32, kind="ExternalInput").ap()
    bg1 = dt("bg1", [DV, 1], F32, kind="ExternalInput").ap()
    bg2 = dt("bg2", [HC * DV, 1], F32, kind="ExternalInput").ap()
    onormw = dt("onormw", [DV, 1], F32, kind="ExternalInput").ap()
    identc = dt("identc", [128, 128], F32, kind="ExternalInput").ap()
    maskMc = dt("maskMc", [128, 128], F32, kind="ExternalInput").ap()
    maskMTc = dt("maskMTc", [128, 128], F32, kind="ExternalInput").ap()
    maskATc = dt("maskATc", [128, C], F32, kind="ExternalInput").ap()
    onesr = dt("onesr", [1, 128], F32, kind="ExternalInput").ap()
    onescl = dt("onescl", [128, 1], F32, kind="ExternalInput").ap()
    ogated = dt("ogated", [HC * DV, T], F32, kind="ExternalOutput").ap()
    betad = dt("betad", [HC * R, BLK], BF16).ap()

    NPLANE = HC * (1 + 2 * R)

    es = contextlib.ExitStack()
    xblk = es.enter_context(nc.sbuf_tensor("xblk", [128, 16 * BLK], BF16))
    wslab0 = es.enter_context(nc.sbuf_tensor("wslab0", [128, 16 * 128], BF16))
    wslab1 = es.enter_context(nc.sbuf_tensor("wslab1", [128, 16 * 128], BF16))
    wf2sb = es.enter_context(nc.sbuf_tensor("wf2sb", [128, HC * DK], BF16))
    wg2sb = es.enter_context(nc.sbuf_tensor("wg2sb", [128, HC * DV], BF16))
    raw0 = es.enter_context(nc.sbuf_tensor("raw0", [128, BLK + 3], BF16))
    raw1 = es.enter_context(nc.sbuf_tensor("raw1", [128, BLK + 3], BF16))
    halo = es.enter_context(nc.sbuf_tensor("halo", [128, 3 * NPLANE], BF16))
    qf = es.enter_context(nc.sbuf_tensor("qf", [128, HC * BLK], BF16))
    kf = es.enter_context(nc.sbuf_tensor("kf", [128, HC * R * BLK], BF16))
    vf = es.enter_context(nc.sbuf_tensor("vf", [128, HC * R * BLK], BF16))
    gate = es.enter_context(nc.sbuf_tensor("gate", [128, HC * BLK], BF16))
    Ghat = es.enter_context(nc.sbuf_tensor("Ghat", [128, HC * (BLK + 1)], F32))
    carry = es.enter_context(nc.sbuf_tensor("carry", [128, HC], F32))
    laS = es.enter_context(nc.sbuf_tensor("laS", [128, BLK], F32))
    sqS = es.enter_context(nc.sbuf_tensor("sqS", [128, BLK], BF16))
    f1sb = es.enter_context(nc.sbuf_tensor("f1sb", [128, BLK], BF16))
    g1sb = es.enter_context(nc.sbuf_tensor("g1sb", [128, BLK], BF16))
    betast = es.enter_context(nc.sbuf_tensor("betast", [HC * R, BLK], BF16))
    betafl = es.enter_context(nc.sbuf_tensor("betafl", [1, R * BLK], BF16))
    Awide = es.enter_context(nc.sbuf_tensor("Awide", [128, R * BLK], BF16))
    Bwide = es.enter_context(nc.sbuf_tensor("Bwide", [128, R * BLK], BF16))
    qhatw = es.enter_context(nc.sbuf_tensor("qhatw", [128, BLK], BF16))
    expCm = es.enter_context(nc.sbuf_tensor("expCm", [128, CPB], F32))
    cmat = es.enter_context(nc.sbuf_tensor("cmat", [128, CPB], F32))
    gammat = es.enter_context(nc.sbuf_tensor("gammat", [128, CPB], F32))
    anch = es.enter_context(nc.sbuf_tensor("anch", [128, CPB], F32))
    Bop = es.enter_context(nc.sbuf_tensor("Bop", [128, 128], BF16))
    Mm = es.enter_context(nc.sbuf_tensor("Mm", [128, 128], BF16))
    MTm = es.enter_context(nc.sbuf_tensor("MTm", [128, 128], BF16))
    Pk = es.enter_context(nc.sbuf_tensor("Pk", [128, 6 * 128], BF16))
    Qk = es.enter_context(nc.sbuf_tensor("Qk", [128, 5 * 128], BF16))
    Xt = es.enter_context(nc.sbuf_tensor("Xt", [128, 2 * 128], BF16))
    attnTm = es.enter_context(nc.sbuf_tensor("attnTm", [128, C], BF16))
    vmm = es.enter_context(nc.sbuf_tensor("vmm", [128, 128], BF16))
    wpf = es.enter_context(nc.sbuf_tensor("wpf", [128, 128], BF16))
    wpm = es.enter_context(nc.sbuf_tensor("wpm", [128, 128], BF16))
    Shat = es.enter_context(nc.sbuf_tensor("Shat", [128, 128], BF16))
    rhs_t = es.enter_context(nc.sbuf_tensor("rhs_t", [128, 128], BF16))
    ut = es.enter_context(nc.sbuf_tensor("ut", [128, 128], BF16))
    Sst = es.enter_context(nc.sbuf_tensor("Sst", [128, HC * 128], F32))
    ostage = es.enter_context(nc.sbuf_tensor("ostage", [128, HC * BLK], BF16))
    ogout = es.enter_context(nc.sbuf_tensor("ogout", [128, BLK], F32))
    rowf = es.enter_context(nc.sbuf_tensor("rowf", [1, 512], F32))
    rowb = es.enter_context(nc.sbuf_tensor("rowb", [1, 512], BF16))
    identb = es.enter_context(nc.sbuf_tensor("identb", [128, 128], BF16))
    maskMb = es.enter_context(nc.sbuf_tensor("maskMb", [128, 128], BF16))
    maskMTb = es.enter_context(nc.sbuf_tensor("maskMTb", [128, 128], BF16))
    maskATb = es.enter_context(nc.sbuf_tensor("maskATb", [128, C], BF16))
    ones1b = es.enter_context(nc.sbuf_tensor("ones1b", [1, 128], BF16))
    ones1f = es.enter_context(nc.sbuf_tensor("ones1f", [1, 128], F32))
    onescb = es.enter_context(nc.sbuf_tensor("onescb", [128, 1], BF16))
    convq_s = es.enter_context(nc.sbuf_tensor("convq_s", [128, 4 * HC], F32))
    convk_s = es.enter_context(nc.sbuf_tensor("convk_s", [128, 4 * HC * R], F32))
    convv_s = es.enter_context(nc.sbuf_tensor("convv_s", [128, 4 * HC * R], F32))
    dtb_s = es.enter_context(nc.sbuf_tensor("dtb_s", [128, HC], F32))
    bg1_s = es.enter_context(nc.sbuf_tensor("bg1_s", [128, 1], F32))
    bg2_s = es.enter_context(nc.sbuf_tensor("bg2_s", [128, HC], F32))
    onorm_s = es.enter_context(nc.sbuf_tensor("onorm_s", [128, 1], F32))
    onecol = es.enter_context(nc.sbuf_tensor("onecol", [128, 1], F32))
    epscol = es.enter_context(nc.sbuf_tensor("epscol", [1, 1], F32))
    alog_s = es.enter_context(nc.sbuf_tensor("alog_s", [1, HC], F32))
    nega_row = es.enter_context(nc.sbuf_tensor("nega_row", [1, HC], F32))
    nega_b = es.enter_context(nc.sbuf_tensor("nega_b", [128, HC], F32))
    zcol = es.enter_context(nc.sbuf_tensor("zcol", [128, 1], F32))
    pb0 = es.enter_context(nc.psum_tensor("pb0", [128, 512], F32))
    pb1 = es.enter_context(nc.psum_tensor("pb1", [128, 512], F32))
    pbc = es.enter_context(nc.psum_tensor("pbc", [128, 512], F32))
    pbm = es.enter_context(nc.psum_tensor("pbm", [128, 512], F32))
    pbs0 = es.enter_context(nc.psum_tensor("pbs0", [128, 512], F32))
    pbs1 = es.enter_context(nc.psum_tensor("pbs1", [128, 512], F32))
    pbl = es.enter_context(nc.psum_tensor("pbl", [128, 512], F32))
    pbt = es.enter_context(nc.psum_tensor("pbt", [128, 256], BF16))
    s_ld = es.enter_context(nc.semaphore("ld"))
    s_pe = es.enter_context(nc.semaphore("pe"))
    s_dv = es.enter_context(nc.semaphore("dv"))
    s_ac = es.enter_context(nc.semaphore("ac"))
    s_po = es.enter_context(nc.semaphore("po"))
    s_st = es.enter_context(nc.semaphore("st"))
    if True:
        sems = {"ld": s_ld, "pe": s_pe, "dv": s_dv, "ac": s_ac, "po": s_po, "st": s_st}
        p = Plan()

        def DMA(eng, out, in_, waits=(), slow=False):
            return p.emit(eng, lambda e, o=out, i=in_, sl=slow: e.dma_start(
                out=o, in_=i, allow_slow_non_contiguous=sl),
                waits=waits, inc="ld", amt=16)

        def ST(out, in_, waits=()):
            return p.emit("sync", lambda e, o=out, i=in_: e.dma_start(out=o, in_=i),
                          waits=waits, inc="st", amt=16)

        for dst, src in [
            (identb[:, :], identc[:, :]), (maskMb[:, :], maskMc[:, :]),
            (maskMTb[:, :], maskMTc[:, :]), (maskATb[:, :], maskATc[:, :]),
            (ones1b[:, :], onesr[:, :]), (onescb[:, :], onescl[:, :]),
            (ones1f[:, :], onesr[:, :]),
            (convq_s[:, :].rearrange("p (m j) -> p m j", j=4), convq[:, :].rearrange("(m p) j -> p m j", p=128)),
            (convk_s[:, :].rearrange("p (m j) -> p m j", j=4), convk[:, :].rearrange("(m p) j -> p m j", p=128)),
            (convv_s[:, :].rearrange("p (m j) -> p m j", j=4), convv[:, :].rearrange("(m p) j -> p m j", p=128)),
            (dtb_s[:, :].unsqueeze(2), dtb[:, :].rearrange("(m p) one -> p m one", p=128)),
            (bg1_s[:, :], bg1[:, :]),
            (bg2_s[:, :].unsqueeze(2), bg2[:, :].rearrange("(m p) one -> p m one", p=128)),
            (onorm_s[:, :], onormw[:, :]),
            (alog_s[:, :], alogr[:, :]),
            (wf2sb[:, :], wf2T[:, :]), (wg2sb[:, :], wg2T[:, :]),
        ]:
            DMA("gpsimd", dst, src, slow=True)
        ld_pre = p.val("ld")
        p.emit("vector", lambda e: e.memset(onecol[:, :], 1.0), inc="dv")
        p.emit("vector", lambda e: e.memset(epscol[:, :], NORM_EPS), inc="dv")
        p.emit("vector", lambda e: e.memset(zcol[:, :], 0.0), inc="dv")
        p.emit("vector", lambda e: e.memset(carry[:, :], 0.0), inc="dv")
        p.emit("vector", lambda e: e.memset(Sst[:, :], 0.0), inc="dv")
        p.emit("vector", lambda e: e.memset(halo[:, :], 0.0), inc="dv")
        p.emit("scalar", lambda e: e.activation(nega_row[:, :], alog_s[:, :], AF.Exp),
               waits=[("ld", ld_pre)], inc="ac")
        p.emit("vector", lambda e: e.tensor_scalar(nega_row[:, :], nega_row[:, :], -1.0, None, ALU.mult),
               waits=[("ac", p.val("ac"))], inc="dv")
        p.emit("tensor", lambda e: e.matmul(pbc[0:128, 0:HC], ones1f[:, :], nega_row[:, :],
                                            start=True, stop=True),
               waits=[("dv", p.val("dv")), ("ld", ld_pre)], inc="pe")
        p.emit("vector", lambda e: e.tensor_copy(nega_b[:, :], pbc[0:128, 0:HC]),
               waits=[("pe", p.val("pe"))], inc="dv")
        pre_dv = p.val("dv")

        planes = []
        for m in range(4):
            planes.append(("q", qf, m * BLK, convq_s, 4 * m))
        for m in range(16):
            planes.append(("k", kf, m * BLK, convk_s, 4 * m))
        for m in range(16):
            planes.append(("v", vf, m * BLK, convv_s, 4 * m))

        def wsrc(m):
            if m < 4:
                return wqT[:, m * 128:(m + 1) * 128], 128
            if m < 20:
                return wkT[:, (m - 4) * 128:(m - 3) * 128], 128
            if m < 36:
                return wvT[:, (m - 20) * 128:(m - 19) * 128], 128
            if m == 36:
                return wbT[:, :], 16
            if m == 37:
                return wf1T[:, :], 128
            return wg1T[:, :], 128

        NM = 39
        slabs = [wslab0, wslab1]
        slab_free = [("pe", 0), ("pe", 0)]
        conv_done = {}

        for blk_i in range(NBLK):
            t0 = blk_i * BLK
            xsrc = xT[:, t0:t0 + BLK].rearrange("(kc kp) t -> kp kc t", kp=128)
            DMA("gpsimd", xblk[:, :].rearrange("p (kc t) -> p kc t", kc=16), xsrc, waits=[("pe", p.val("pe"))])
            ld_x = p.val("ld")

            src0, w0 = wsrc(0)
            DMA("gpsimd", slabs[0][:, 0:16 * w0].rearrange("p (kc mm) -> p kc mm", kc=16),
                src0.rearrange("(kc kp) mm -> kp kc mm", kp=128), waits=[slab_free[0]])
            ld_slab = [p.val("ld"), 0]

            raw_done = {}
            ps_free = {0: ("dv", 0), 1: ("dv", 0)}
            raw_free = {0: ("dv", 0), 1: ("dv", 0)}
            for m in range(NM):
                cur = m % 2
                width = wsrc(m)[1]
                if m + 1 < NM:
                    srcn, wn = wsrc(m + 1)
                    DMA("gpsimd", slabs[1 - cur][:, 0:16 * wn].rearrange("p (kc mm) -> p kc mm", kc=16),
                        srcn.rearrange("(kc kp) mm -> kp kc mm", kp=128),
                        waits=[slab_free[1 - cur]])
                    ld_slab[1 - cur] = p.val("ld")
                slab = slabs[cur]
                is_plane = m < 36
                raw = (raw0, raw1)[m % 2] if is_plane else None
                for th in range(2):
                    ps = (pb0, pb1)[th]
                    mmw = [("ld", max(ld_x, ld_slab[cur])), ps_free[th]]
                    if th == 0:
                        mmw += p.bar()
                    if is_plane:
                        mmw.append(raw_free[m % 2])
                    for kc in range(16):
                        w_ap = slab[:, kc * width:(kc + 1) * width]
                        x_ap = xblk[:, kc * BLK + th * 512: kc * BLK + th * 512 + 512]
                        p.emit("tensor",
                               lambda e, o=ps[0:width, 0:512], w=w_ap, x=x_ap,
                               a=(kc == 0), b=(kc == 15): e.matmul(o, w, x, start=a, stop=b),
                               waits=mmw if kc == 0 else (), inc="pe" if kc == 15 else None)
                    pe_mm = p.val("pe")
                    if is_plane:
                        dst = raw[:, 3 + th * 512: 3 + th * 512 + 512]
                        if (m + th) % 2 == 0:
                            p.emit("vector", lambda e, d=dst, s=ps: e.tensor_copy(d, s[0:128, 0:512]),
                                   waits=[("pe", pe_mm), raw_free[m % 2]], inc="dv")
                            raw_done[(m, th)] = ("dv", p.val("dv"))
                            ps_free[th] = ("dv", p.val("dv"))
                        else:
                            p.emit("scalar", lambda e, d=dst, s=ps: e.copy(d, s[0:128, 0:512]),
                                   waits=[("pe", pe_mm), raw_free[m % 2]], inc="ac")
                            raw_done[(m, th)] = ("ac", p.val("ac"))
                            ps_free[th] = ("ac", p.val("ac"))
                    elif m == 36:
                        p.emit("scalar", lambda e, d=betast[:, th * 512: th * 512 + 512], s=ps:
                               e.activation(d, s[0:16, 0:512], AF.Sigmoid),
                               waits=[("pe", pe_mm), ("st", p.val("st"))], inc="ac")
                        raw_done[(m, th)] = ("ac", p.val("ac"))
                        ps_free[th] = ("ac", p.val("ac"))
                    elif m == 37:
                        p.emit("vector", lambda e, d=f1sb[:, th * 512: th * 512 + 512], s=ps:
                               e.tensor_copy(d, s[0:128, 0:512]),
                               waits=[("pe", pe_mm)], inc="dv")
                        raw_done[(m, th)] = ("dv", p.val("dv"))
                        ps_free[th] = ("dv", p.val("dv"))
                    else:
                        p.emit("vector", lambda e, d=g1sb[:, th * 512: th * 512 + 512], s=ps:
                               e.tensor_scalar(d, s[0:128, 0:512], bg1_s[:, 0:1], None, ALU.add),
                               waits=[("pe", pe_mm)], inc="dv")
                        raw_done[(m, th)] = ("dv", p.val("dv"))
                        ps_free[th] = ("dv", p.val("dv"))
                slab_free[cur] = ("pe", p.val("pe"))
                if is_plane:
                    kind, fin, foff, cw, cwoff = planes[m]
                    both = [raw_done[(m, 0)], raw_done[(m, 1)]]
                    p.emit("vector", lambda e, r=raw, pl=m: e.tensor_copy(r[:, 0:3], halo[:, 3 * pl:3 * pl + 3]),
                           waits=both, inc="dv")
                    p.emit("vector", lambda e, r=raw, pl=m: e.tensor_copy(halo[:, 3 * pl:3 * pl + 3],
                                                                          r[:, BLK:BLK + 3]),
                           waits=both, inc="dv")
                    hv = ("dv", p.val("dv"))
                    if kind == "q":
                        dst = fin[:, foff:foff + BLK]
                        dst3 = dst.rearrange("p (n t) -> p n t", n=CPB)
                    else:
                        hh2, ii2 = divmod(m - (4 if kind == "k" else 20), 4)
                        base = hh2 * (R * BLK) + ii2 * C
                        dst3 = fin[:, base:base + R * BLK - ii2 * C if False else None]
                        dst3 = fin[:, :].rearrange("p (hh n i t) -> p hh n i t", hh=HC, n=CPB, i=R)[
                            :, hh2, :, ii2, :]
                        dst = dst3
                    eng, sem = ("vector", "dv")
                    p.emit(eng, lambda e, d=dst3, r=raw, w=cw[:, cwoff:cwoff + 1]:
                           e.tensor_scalar(d, r[:, 0:BLK].rearrange("p (n t) -> p n t", n=CPB),
                                           w, None, ALU.mult),
                           waits=[hv, ("ld", ld_pre), ("pe", p.val("pe"))], inc=sem)
                    for j in range(1, 4):
                        p.emit(eng, lambda e, d=dst3, r=raw, w=cw[:, cwoff + j:cwoff + j + 1], jj=j:
                               e.scalar_tensor_tensor(
                                   d, r[:, jj:jj + BLK].rearrange("p (n t) -> p n t", n=CPB),
                                   w, d, ALU.mult, ALU.add),
                               inc=sem)
                    cv = (sem, p.val(sem))
                    raw_free[m % 2] = cv
                    p.emit("scalar", lambda e, d=dst3: e.activation(d, d, AF.Silu),
                           waits=[cv], inc="ac")
                    conv_done[m] = ("ac", p.val("ac"))

            for pl in range(20):
                kind, fin, foff, _, _ = planes[pl]
                if kind == "q":
                    dst = fin[:, foff:foff + BLK]
                else:
                    hh2, ii2 = divmod(pl - 4, 4)
                    dst = fin[:, :].rearrange("p (hh n i t) -> p hh n i t", hh=HC, n=CPB, i=R)[
                        :, hh2, :, ii2, :]
                p.emit("scalar", lambda e, d=dst: e.activation(
                    sqS[:, :].rearrange("p (n t) -> p n t", n=CPB) if len(d.shape) == 3 else sqS[:, :],
                    d, AF.Square),
                       waits=[conv_done[pl]] + p.bar(), inc="ac")
                sq_done = ("ac", p.val("ac"))
                for th in range(2):
                    p.emit("tensor", lambda e, t=th: e.matmul(pbs0[0:1, 0:512], onescb[:, :],
                                                              sqS[:, t * 512:t * 512 + 512],
                                                              start=True, stop=True),
                           waits=[sq_done, ("ac", p.val("ac")), ("dv", p.val("dv"))], inc="pe")
                    mmv = ("pe", p.val("pe"))
                    p.emit("vector", lambda e: e.tensor_scalar(rowf[:, 0:512], pbs0[0:1, 0:512],
                                                               1e-24, None, ALU.max),
                           waits=[mmv], inc="dv")
                    p.emit("scalar", lambda e: e.activation(rowf[:, 0:512], rowf[:, 0:512], AF.Ln),
                           waits=[("dv", p.val("dv"))], inc="ac")
                    p.emit("scalar", lambda e: e.activation(rowb[:, 0:512], rowf[:, 0:512], AF.Exp,
                                                            scale=-0.5), inc="ac")
                    rr = ("ac", p.val("ac"))
                    p.emit("tensor", lambda e: e.matmul(pbc[0:128, 0:512], ones1b[:, :],
                                                        rowb[:, 0:512], start=True, stop=True),
                           waits=[rr], inc="pe")
                    if len(dst.shape) == 3:
                        dsl = dst[:, th * 16:(th + 1) * 16, :]
                        psl = pbc[0:128, 0:512].rearrange("p (n t) -> p n t", n=16)
                    else:
                        dsl = dst[:, th * 512:th * 512 + 512]
                        psl = pbc[0:128, 0:512]
                    p.emit("vector", lambda e, d=dsl, s2=psl: e.tensor_tensor(d, d, s2, ALU.mult),
                           waits=[("pe", p.val("pe"))], inc="dv")
                conv_done[pl] = ("dv", p.val("dv"))

            ST(betad[:, :], betast[:, :], waits=[raw_done[(36, 0)], raw_done[(36, 1)]])
            beta_st = ("st", p.val("st"))

            fg_done = [raw_done[(37, 0)], raw_done[(37, 1)], raw_done[(38, 0)], raw_done[(38, 1)]]
            for h in range(HC):
                for th in range(2):
                    p.emit("tensor", lambda e, hh=h, t=th: e.matmul(
                        pbm[0:128, 0:512], wf2sb[:, hh * DK:(hh + 1) * DK],
                        f1sb[:, t * 512:t * 512 + 512], start=True, stop=True),
                        waits=fg_done + [("ld", ld_pre)] + p.bar(),
                        inc="pe")
                    mmv = ("pe", p.val("pe"))
                    p.emit("scalar", lambda e, hh=h, t=th: e.activation(
                        laS[:, t * 512:t * 512 + 512], pbm[0:128, 0:512], AF.Exp,
                        bias=dtb_s[:, hh:hh + 1], scale=1.0), waits=[mmv], inc="ac")
                    p.emit("scalar", lambda e, t=th: e.activation(
                        laS[:, t * 512:t * 512 + 512], laS[:, t * 512:t * 512 + 512], AF.Ln,
                        bias=onecol[:, 0:1], scale=1.0), inc="ac")
                    p.emit("vector", lambda e, hh=h, t=th: e.tensor_scalar(
                        laS[:, t * 512:t * 512 + 512], laS[:, t * 512:t * 512 + 512],
                        nega_b[:, hh:hh + 1], None, ALU.mult),
                        waits=[("ac", p.val("ac")), ("dv", pre_dv)], inc="dv")
                gh = Ghat[:, h * (BLK + 1):(h + 1) * (BLK + 1)]
                p.emit("vector", lambda e, g=gh, hh=h: e.tensor_copy(g[:, 0:1], carry[:, hh:hh + 1]),
                       inc="dv")
                p.emit("vector", lambda e, g=gh, hh=h: e.tensor_tensor_scan(
                    g[:, 1:BLK + 1], laS[:, :], zcol[:, 0:1].broadcast_to([128, BLK]),
                    carry[:, hh:hh + 1], ALU.add, ALU.add), inc="dv")
                p.emit("vector", lambda e, g=gh, hh=h: e.tensor_copy(carry[:, hh:hh + 1],
                                                                     g[:, BLK:BLK + 1]), inc="dv")
                for th in range(2):
                    p.emit("tensor", lambda e, hh=h, t=th: e.matmul(
                        pbm[0:128, 0:512], wg2sb[:, hh * DV:(hh + 1) * DV],
                        g1sb[:, t * 512:t * 512 + 512], start=True, stop=True),
                        waits=[("ac", p.val("ac")), ("dv", p.val("dv"))], inc="pe")
                    p.emit("scalar", lambda e, hh=h, t=th: e.activation(
                        gate[:, hh * BLK + t * 512: hh * BLK + t * 512 + 512],
                        pbm[0:128, 0:512], AF.Sigmoid, bias=bg2_s[:, hh:hh + 1], scale=1.0),
                        waits=[("pe", p.val("pe"))], inc="ac")

            for h in range(HC):
                DMA("gpsimd", betafl[:, :].rearrange("one (n i t) -> one n i t", n=CPB, i=R),
                    betad[h * R:(h + 1) * R, :].rearrange("i (n t) -> n i t", n=CPB).unsqueeze(0),
                    waits=[beta_st, ("pe", p.val("pe"))], slow=True)
                ld_beta = p.val("ld")
                gh = Ghat[:, h * (BLK + 1):(h + 1) * (BLK + 1)]
                gall = gh[:, 1:BLK + 1].rearrange("p (c t) -> p c t", c=CPB)
                miA = gh[:, C // 2::C]
                pA = gh[:, 0::C]
                eA = gh[:, C::C]
                scan_pre_w = p.bar()
                p.emit("vector", lambda e, g=gall, mi=miA: e.tensor_tensor(
                    laS[:, :].rearrange("p (c t) -> p c t", c=CPB), g,
                    mi[:, 0:CPB].unsqueeze(2).broadcast_to([128, CPB, C]), ALU.subtract),
                    waits=scan_pre_w, inc="dv")
                p.emit("scalar", lambda e: e.activation(sqS[:, :], laS[:, :], AF.Exp),
                       waits=[("dv", p.val("dv"))], inc="ac")
                exA = ("ac", p.val("ac"))
                p.emit("vector", lambda e, hh=h: e.tensor_tensor(
                    qhatw[:, :], qf[:, hh * BLK:(hh + 1) * BLK], sqS[:, :], ALU.mult),
                    waits=[exA], inc="dv")
                for i2 in range(R):
                    p.emit("vector", lambda e, hh=h, i3=i2: e.tensor_tensor(
                        Awide[:, :].rearrange("p (n i t) -> p n i t", n=CPB, i=R)[:, :, i3, :],
                        kf[:, :].rearrange("p (hh n i t) -> p hh n i t", hh=HC, n=CPB, i=R)[:, hh, :, i3, :],
                        sqS[:, :].rearrange("p (n t) -> p n t", n=CPB), ALU.mult), inc="dv")
                aw_done = ("dv", p.val("dv"))
                p.emit("vector", lambda e, g=gall, mi=miA: e.tensor_tensor(
                    laS[:, :].rearrange("p (c t) -> p c t", c=CPB),
                    mi[:, 0:CPB].unsqueeze(2).broadcast_to([128, CPB, C]), g, ALU.subtract),
                    waits=[aw_done], inc="dv")
                p.emit("scalar", lambda e: e.activation(sqS[:, :], laS[:, :], AF.Exp),
                       waits=[("dv", p.val("dv"))], inc="ac")
                exB = ("ac", p.val("ac"))
                for i2 in range(R):
                    p.emit("vector", lambda e, hh=h, i3=i2: e.tensor_tensor(
                        Bwide[:, :].rearrange("p (n i t) -> p n i t", n=CPB, i=R)[:, :, i3, :],
                        kf[:, :].rearrange("p (hh n i t) -> p hh n i t", hh=HC, n=CPB, i=R)[:, hh, :, i3, :],
                        sqS[:, :].rearrange("p (n t) -> p n t", n=CPB), ALU.mult),
                        waits=[exB] if i2 == 0 else (), inc="dv")
                p.emit("vector", lambda e, miv=miA, ev=eA: e.tensor_tensor(
                    anch[:, :], ev[:, 0:CPB], miv[:, 0:CPB], ALU.subtract), inc="dv")
                p.emit("scalar", lambda e: e.activation(expCm[:, :], anch[:, :], AF.Exp),
                       waits=[("dv", p.val("dv"))], inc="ac")
                p.emit("vector", lambda e, miv=miA, pv=pA: e.tensor_tensor(
                    anch[:, :], miv[:, 0:CPB], pv[:, 0:CPB], ALU.subtract),
                    waits=[("ac", p.val("ac"))], inc="dv")
                p.emit("scalar", lambda e: e.activation(cmat[:, :], anch[:, :], AF.Exp),
                       waits=[("dv", p.val("dv"))], inc="ac")
                p.emit("vector", lambda e, ev=eA, pv=pA: e.tensor_tensor(
                    anch[:, :], ev[:, 0:CPB], pv[:, 0:CPB], ALU.subtract),
                    waits=[("ac", p.val("ac"))], inc="dv")
                p.emit("scalar", lambda e: e.activation(gammat[:, :], anch[:, :], AF.Exp),
                       waits=[("dv", p.val("dv"))], inc="ac")
                pre_done = [("ac", p.val("ac")), ("dv", p.val("dv"))]

                for n in range(CPB):
                    A3 = Awide[:, n * 128:(n + 1) * 128]
                    B3 = Bwide[:, n * 128:(n + 1) * 128]
                    V3 = vf[:, h * R * BLK + n * 128: h * R * BLK + (n + 1) * 128]
                    qsl = qhatw[:, n * C:(n + 1) * C]
                    bsl = betafl[:, n * 128:(n + 1) * 128]
                    p.emit("tensor", lambda e, b=bsl: e.matmul(pbc[0:128, 0:128], ones1b[:, :], b,
                                                               start=True, stop=True),
                           waits=(pre_done if n == 0 else []) + [("ld", ld_beta)] + p.bar(),
                           inc="pe")
                    p.emit("vector", lambda e, b=B3: e.tensor_tensor(
                        Bop[:, :], b, pbc[0:128, 0:128], ALU.mult),
                        waits=[("pe", p.val("pe"))], inc="dv")
                    bop_done = ("dv", p.val("dv"))
                    p.emit("tensor", lambda e, a=A3: e.matmul(pb0[0:128, 0:128], a, Bop[:, :],
                                                              start=True, stop=True),
                           waits=[bop_done], inc="pe")
                    p.emit("vector", lambda e: e.tensor_tensor(Mm[:, :], pb0[0:128, 0:128],
                                                               maskMb[:, :], ALU.mult),
                           waits=[("pe", p.val("pe"))], inc="dv")
                    p.emit("tensor", lambda e, a=A3: e.matmul(pb1[0:128, 0:128], Bop[:, :], a,
                                                              start=True, stop=True), inc="pe")
                    p.emit("vector", lambda e: e.tensor_tensor(MTm[:, :], pb1[0:128, 0:128],
                                                               maskMTb[:, :], ALU.mult),
                           waits=[("pe", p.val("pe"))], inc="dv")
                    m_done = ("dv", p.val("dv"))
                    p.emit("gpsimd", lambda e: e.tensor_tensor(Xt[:, 0:128], identb[:, :],
                                                               MTm[:, :], ALU.subtract),
                           waits=[m_done, ("pe", p.val("pe"))], inc="po")
                    x_cur = 0
                    x_done = ("po", p.val("po"))
                    prevP_ap, prevQ_ap = Mm[:, :], MTm[:, :]
                    for k in range(1, 7):
                        pk = Pk[:, (k - 1) * 128:k * 128]
                        p.emit("tensor", lambda e, l=prevQ_ap, r2=prevP_ap: e.matmul(
                            pbs0[0:128, 0:128], l, r2, start=True, stop=True),
                            waits=[m_done] if k == 1 else [("ac", p.val("ac"))], inc="pe")
                        p.emit("scalar", lambda e, d=pk: e.copy(d, pbs0[0:128, 0:128]),
                               waits=[("pe", p.val("pe"))], inc="ac")
                        pk_done = ("ac", p.val("ac"))
                        if k < 6:
                            qk = Qk[:, (k - 1) * 128:k * 128]
                            p.emit("tensor", lambda e, l=prevP_ap, r2=prevQ_ap: e.matmul(
                                pbs1[0:128, 0:128], l, r2, start=True, stop=True), inc="pe")
                            p.emit("scalar", lambda e, d=qk: e.copy(d, pbs1[0:128, 0:128]),
                                   waits=[("pe", p.val("pe"))], inc="ac")
                            prevQ_ap = qk
                        p.emit("tensor", lambda e, l=pk, x=Xt[:, x_cur * 128:(x_cur + 1) * 128]:
                               e.matmul(pbl[0:128, 0:128], l, x, start=True, stop=True),
                               waits=[pk_done, x_done], inc="pe")
                        p.emit("vector", lambda e, xo=Xt[:, x_cur * 128:(x_cur + 1) * 128],
                               xn=Xt[:, (1 - x_cur) * 128:(2 - x_cur) * 128]:
                               e.tensor_tensor(xn, xo, pbl[0:128, 0:128], ALU.add),
                               waits=[("pe", p.val("pe"))], inc="dv")
                        x_cur = 1 - x_cur
                        x_done = ("dv", p.val("dv"))
                        prevP_ap = pk
                    TT_ap = Xt[:, x_cur * 128:(x_cur + 1) * 128]
                    p.emit("tensor", lambda e, q=qsl: e.matmul(pb0[0:128, 0:C], Bop[:, :], q,
                                                               start=True, stop=True),
                           waits=[("dv", p.val("dv"))], inc="pe")
                    p.emit("vector", lambda e: e.tensor_tensor(attnTm[:, :], pb0[0:128, 0:C],
                                                               maskATb[:, :], ALU.mult),
                           waits=[("pe", p.val("pe"))], inc="dv")
                    at_done = ("dv", p.val("dv"))
                    p.emit("tensor", lambda e, v=V3: e.transpose(pbt[0:128, 0:128], v, identb[:, :]),
                           waits=[at_done], inc="pe")
                    p.emit("scalar", lambda e: e.copy(vmm[:, :], pbt[0:128, 0:128]),
                           waits=[("pe", p.val("pe")), ("dv", p.val("dv"))], inc="ac")
                    vmm_done = ("ac", p.val("ac"))
                    p.emit("vector", lambda e, nn=n: e.tensor_scalar(wpf[:, :], Bop[:, :],
                                                                     expCm[:, nn:nn + 1], None,
                                                                     ALU.mult),
                           waits=[bop_done], inc="dv")
                    p.emit("tensor", lambda e: e.transpose(pbt[0:128, 128:256], wpf[:, :],
                                                           identb[:, :]),
                           waits=[("dv", p.val("dv")), vmm_done], inc="pe")
                    p.emit("scalar", lambda e: e.copy(wpm[:, :], pbt[0:128, 128:256]),
                           waits=[("pe", p.val("pe"))], inc="ac")
                    wpm_done = ("ac", p.val("ac"))
                    Sh = Sst[:, h * 128:(h + 1) * 128]
                    p.emit("vector", lambda e, s=Sh, nn=n: e.tensor_scalar(
                        Shat[:, :], s, cmat[:, nn:nn + 1], None, ALU.mult),
                        waits=[("pe", p.val("pe"))], inc="dv")
                    sh_done = ("dv", p.val("dv"))
                    p.emit("tensor", lambda e, a=A3: e.matmul(pb1[0:128, 0:128], a, Shat[:, :],
                                                              start=True, stop=True),
                           waits=[sh_done], inc="pe")
                    p.emit("vector", lambda e: e.tensor_tensor(rhs_t[:, :], vmm[:, :],
                                                               pb1[0:128, 0:128], ALU.subtract),
                           waits=[("pe", p.val("pe")), vmm_done], inc="dv")
                    p.emit("tensor", lambda e, x=TT_ap: e.matmul(pbl[0:128, 0:128], x, rhs_t[:, :],
                                                                 start=True, stop=True),
                           waits=[("dv", p.val("dv"))], inc="pe")
                    p.emit("vector", lambda e: e.tensor_copy(ut[:, :], pbl[0:128, 0:128]),
                           waits=[("pe", p.val("pe"))], inc="dv")
                    ut_done = ("dv", p.val("dv"))
                    p.emit("tensor", lambda e, q=qsl: e.matmul(pb0[0:128, 0:C], Shat[:, :], q,
                                                               start=True, stop=False),
                           waits=[ut_done])
                    p.emit("tensor", lambda e: e.matmul(pb0[0:128, 0:C], ut[:, :], attnTm[:, :],
                                                        start=False, stop=True), inc="pe")
                    p.emit("vector", lambda e, hh=h, nn=n: e.tensor_scalar(
                        ostage[:, hh * BLK + nn * C: hh * BLK + (nn + 1) * C],
                        pb0[0:128, 0:C], SCALE, None, ALU.mult),
                        waits=[("pe", p.val("pe"))], inc="dv")
                    p.emit("tensor", lambda e: e.matmul(pbm[0:128, 0:128], wpm[:, :], ut[:, :],
                                                        start=True, stop=True),
                           waits=[wpm_done, ut_done], inc="pe")
                    p.emit("vector", lambda e, s=Sh, nn=n: e.scalar_tensor_tensor(
                        s, s, gammat[:, nn:nn + 1], pbm[0:128, 0:128], ALU.mult, ALU.add),
                        waits=[("pe", p.val("pe"))], inc="dv")

                oh = ostage[:, h * BLK:(h + 1) * BLK]
                p.emit("scalar", lambda e, o=oh: e.activation(sqS[:, :], o, AF.Square),
                       waits=p.bar(), inc="ac")
                sq_done = ("ac", p.val("ac"))
                for th in range(2):
                    p.emit("tensor", lambda e, t=th: e.matmul(pbs0[0:1, 0:512], onescb[:, :],
                                                              sqS[:, t * 512:t * 512 + 512],
                                                              start=True, stop=True),
                           waits=[sq_done, ("ac", p.val("ac")), ("dv", p.val("dv"))], inc="pe")
                    p.emit("scalar", lambda e: e.activation(rowf[:, 0:512], pbs0[0:1, 0:512],
                                                            AF.Ln, bias=epscol[:, 0:1],
                                                            scale=1.0 / DV),
                           waits=[("pe", p.val("pe"))], inc="ac")
                    p.emit("scalar", lambda e: e.activation(rowb[:, 0:512], rowf[:, 0:512],
                                                            AF.Exp, scale=-0.5), inc="ac")
                    rr = ("ac", p.val("ac"))
                    p.emit("tensor", lambda e: e.matmul(pbc[0:128, 0:512], ones1b[:, :],
                                                        rowb[:, 0:512], start=True, stop=True),
                           waits=[rr], inc="pe")
                    p.emit("vector", lambda e, o=oh, t=th: e.tensor_tensor(
                        sqS[:, t * 512:t * 512 + 512], o[:, t * 512:t * 512 + 512],
                        pbc[0:128, 0:512], ALU.mult),
                        waits=[("pe", p.val("pe"))], inc="dv")
                    p.emit("vector", lambda e, hh=h, t=th: e.tensor_tensor(
                        sqS[:, t * 512:t * 512 + 512], sqS[:, t * 512:t * 512 + 512],
                        gate[:, hh * BLK + t * 512: hh * BLK + t * 512 + 512], ALU.mult), inc="dv")
                    p.emit("vector", lambda e, t=th: e.tensor_scalar(
                        qhatw[:, t * 512:t * 512 + 512], sqS[:, t * 512:t * 512 + 512],
                        onorm_s[:, 0:1], None, ALU.mult), inc="dv")
                DMA("gpsimd", ogated[h * DV:(h + 1) * DV, t0:t0 + BLK], qhatw[:, :],
                    waits=[("dv", p.val("dv"))])
                p.emit("vector", lambda e: e.engine_nop(), waits=[("ld", p.val("ld"))], inc="dv")

        p.emit("gpsimd", lambda e: e.engine_nop(), waits=[("st", p.val("st"))], inc="po")
        _run_plan(nc, p, sems)
    es.close()
    return nc


def build_post():
    nc = bass.Bass("TRN2", target_bir_lowering=False)
    dt = nc.dram_tensor
    ofull = dt("ofull", [D, TQ], F32, kind="ExternalInput").ap()
    xTs = dt("xTs", [D, TQ], F32, kind="ExternalInput").ap()
    woT = dt("woT", [D, D], F32, kind="ExternalInput").ap()
    wg8 = dt("wg8", [D, 4 * D], I8, kind="ExternalInput").ap()
    wu8 = dt("wu8", [D, 4 * D], I8, kind="ExternalInput").ap()
    wd8 = dt("wd8", [4 * D, D], I8, kind="ExternalInput").ap()
    sgate = dt("sgate", [4 * D, 1], F32, kind="ExternalInput").ap()
    sup = dt("sup", [4 * D, 1], F32, kind="ExternalInput").ap()
    sdown = dt("sdown", [D, 1], F32, kind="ExternalInput").ap()
    onesr = dt("onesr", [1, 128], F32, kind="ExternalInput").ap()
    onescl = dt("onescl", [128, 1], F32, kind="ExternalInput").ap()
    outT = dt("outT", [D, TQ], F32, kind="ExternalOutput").ap()
    dbgT = (dt("dbgT", [D, TQ], F32, kind="ExternalOutput").ap() if DBG_B else None)
    x1d = dt("x1d", [D, TQ], BF16).ap()

    es = contextlib.ExitStack()
    osb = es.enter_context(nc.sbuf_tensor("osb", [128, 16 * TQ], BF16))
    x1p = es.enter_context(nc.sbuf_tensor("x1p", [128, 16 * TQ], BF16))
    woslab0 = es.enter_context(nc.sbuf_tensor("woslab0", [128, 16 * 128], BF16))
    woslab1 = es.enter_context(nc.sbuf_tensor("woslab1", [128, 16 * 128], BF16))
    xtile = es.enter_context(nc.sbuf_tensor("xtile", [128, 512], F32))
    sq2 = es.enter_context(nc.sbuf_tensor("sq2", [128, 2 * 512], BF16))
    x1sl = es.enter_context(nc.sbuf_tensor("x1sl", [128, 16 * 512], BF16))
    wgsl = es.enter_context(nc.sbuf_tensor("wgsl", [128, 16 * 256], BF16))
    wusl = es.enter_context(nc.sbuf_tensor("wusl", [128, 16 * 256], BF16))
    wdsl0 = es.enter_context(nc.sbuf_tensor("wdsl0", [128, 32 * 128], BF16))
    wdsl1 = es.enter_context(nc.sbuf_tensor("wdsl1", [128, 64 * 128], BF16))
    gsb = es.enter_context(nc.sbuf_tensor("gsb", [128, 512], BF16))
    usb = es.enter_context(nc.sbuf_tensor("usb", [128, 512], BF16))
    sgs = es.enter_context(nc.sbuf_tensor("sgs", [128, 64], F32))
    sus = es.enter_context(nc.sbuf_tensor("sus", [128, 64], F32))
    sds = es.enter_context(nc.sbuf_tensor("sds", [128, 16], F32))
    rowm = es.enter_context(nc.sbuf_tensor("rowm", [1, 512], F32))
    rowv = es.enter_context(nc.sbuf_tensor("rowv", [1, 512], F32))
    rowr = es.enter_context(nc.sbuf_tensor("rowr", [1, 512], F32))
    rowmb = es.enter_context(nc.sbuf_tensor("rowmb", [1, 512], BF16))
    rowrb = es.enter_context(nc.sbuf_tensor("rowrb", [1, 512], BF16))
    ones1b = es.enter_context(nc.sbuf_tensor("ones1b", [1, 128], BF16))
    onescb = es.enter_context(nc.sbuf_tensor("onescb", [128, 1], BF16))
    epsc = es.enter_context(nc.sbuf_tensor("epsc", [1, 1], F32))
    qb0 = es.enter_context(nc.psum_tensor("qb0", [128, 512], F32))
    qb1 = es.enter_context(nc.psum_tensor("qb1", [128, 512], F32))
    qbs = es.enter_context(nc.psum_tensor("qbs", [128, 512], F32))
    qbs2 = es.enter_context(nc.psum_tensor("qbs2", [128, 512], F32))
    qbc = es.enter_context(nc.psum_tensor("qbc", [128, 512], F32))
    qbc2 = es.enter_context(nc.psum_tensor("qbc2", [128, 512], F32))
    s_ld = es.enter_context(nc.semaphore("ld"))
    s_pe = es.enter_context(nc.semaphore("pe"))
    s_dv = es.enter_context(nc.semaphore("dv"))
    s_ac = es.enter_context(nc.semaphore("ac"))
    s_po = es.enter_context(nc.semaphore("po"))
    s_st = es.enter_context(nc.semaphore("st"))
    if True:
        sems = {"ld": s_ld, "pe": s_pe, "dv": s_dv, "ac": s_ac, "po": s_po, "st": s_st}
        p = Plan()

        def DMA(eng, out, in_, waits=(), slow=False):
            return p.emit(eng, lambda e, o=out, i=in_, sl=slow: e.dma_start(
                out=o, in_=i, allow_slow_non_contiguous=sl),
                waits=waits, inc="ld", amt=16)

        def ST(out, in_, waits=()):
            return p.emit("sync", lambda e, o=out, i=in_: e.dma_start(out=o, in_=i),
                          waits=waits, inc="st", amt=16)

        DMA("gpsimd", ones1b[:, :], onesr[:, :])
        DMA("gpsimd", onescb[:, :], onescl[:, :])
        DMA("gpsimd", sgs[:, :].unsqueeze(2), sgate[:, :].rearrange("(m p) one -> p m one", p=128), slow=True)
        DMA("gpsimd", sus[:, :].unsqueeze(2), sup[:, :].rearrange("(m p) one -> p m one", p=128), slow=True)
        DMA("gpsimd", sds[:, :].unsqueeze(2), sdown[:, :].rearrange("(m p) one -> p m one", p=128), slow=True)
        DMA("gpsimd", osb[:, :].rearrange("p (kc t) -> p kc t", kc=16),
            ofull[:, :].rearrange("(kc kp) t -> kp kc t", kp=128))
        p.emit("vector", lambda e: e.memset(epsc[:, :], LN_EPS), inc="dv")
        ld0 = p.val("ld")

        ps_free = {0: ("dv", 0), 1: ("dv", 0)}
        xt_free = ("dv", 0)
        for m in range(16):
            DMA("gpsimd", woslab0[:, :].rearrange("p (kc mm) -> p kc mm", kc=16),
                woT[:, m * 128:(m + 1) * 128].rearrange("(kc kp) mm -> kp kc mm", kp=128),
                waits=[("pe", p.val("pe"))])
            ld_slab = p.val("ld")
            for tt in range(4):
                ps = (qb0, qb1)[tt % 2]
                DMA("gpsimd", xtile[:, :], xTs[m * 128:(m + 1) * 128, tt * 512:(tt + 1) * 512],
                    waits=[xt_free] + (p.bar() if tt == 0 else []))
                ldx = p.val("ld")
                for kc in range(16):
                    p.emit("tensor", lambda e, o=ps[0:128, 0:512],
                           w=woslab0[:, kc * 128:(kc + 1) * 128],
                           x=osb[:, kc * TQ + tt * 512: kc * TQ + tt * 512 + 512],
                           a=(kc == 0), b=(kc == 15): e.matmul(o, w, x, start=a, stop=b),
                           waits=([("ld", max(ld0, ld_slab)), ps_free[tt % 2]] +
                                  (p.bar() if tt == 0 else [])) if kc == 0 else (),
                           inc="pe" if kc == 15 else None)
                p.emit("vector", lambda e, d=x1p[:, m * TQ + tt * 512: m * TQ + tt * 512 + 512],
                       s=ps: e.tensor_tensor(d, s[0:128, 0:512], xtile[:, :], ALU.add),
                       waits=[("pe", p.val("pe")), ("ld", ldx)], inc="dv")
                ps_free[tt % 2] = ("dv", p.val("dv"))
                xt_free = ("dv", p.val("dv"))
        phase_done = [("dv", p.val("dv"))]
        if DBG_B == 1:
            DMA("gpsimd", dbgT[:, :].rearrange("(kc kp) t -> kp kc t", kp=128),
                x1p[:, :].rearrange("p (kc t) -> p kc t", kc=16), waits=phase_done)

        def layernorm(xbuf, store_to=None):
            init = list(phase_done)
            for tt in range(4):
                for kc in range(16):
                    p.emit("tensor", lambda e, k=kc, t=tt: e.matmul(
                        qbs[0:1, 0:512], onescb[:, :],
                        xbuf[:, k * TQ + t * 512: k * TQ + t * 512 + 512],
                        start=(k == 0), stop=(k == 15)),
                        waits=(init + [("ld", ld0)] + p.bar()) if kc == 0 else (),
                        inc="pe" if kc == 15 else None)
                sum_done = ("pe", p.val("pe"))
                mm_done = {0: ("pe", 0), 1: ("pe", 0)}
                for kc in range(16):
                    buf = kc % 2
                    p.emit("scalar", lambda e, k=kc, t=tt, b2=buf: e.activation(
                        sq2[:, b2 * 512:b2 * 512 + 512],
                        xbuf[:, k * TQ + t * 512: k * TQ + t * 512 + 512], AF.Square),
                        waits=list(init) + [mm_done[buf]], inc="ac")
                    p.emit("tensor", lambda e, k=kc, b2=buf: e.matmul(
                        qbs2[0:1, 0:512], onescb[:, :], sq2[:, b2 * 512:b2 * 512 + 512],
                        start=(k == 0), stop=(k == 15)),
                        waits=[("ac", p.val("ac"))] + ([("dv", p.val("dv"))] if kc == 0 else []),
                        inc="pe")
                    mm_done[buf] = ("pe", p.val("pe"))
                sq_done = ("pe", p.val("pe"))
                sl = slice(0, 512)
                p.emit("vector", lambda e, s=sl: e.tensor_scalar(rowm[:, s], qbs[0:1, 0:512],
                                                                 1.0 / D, None, ALU.mult),
                       waits=[sum_done], inc="dv")
                p.emit("vector", lambda e, s=sl: e.tensor_scalar(rowv[:, s], qbs2[0:1, 0:512],
                                                                 1.0 / D, None, ALU.mult),
                       waits=[sq_done], inc="dv")
                p.emit("vector", lambda e, s=sl: e.tensor_tensor(rowr[:, s], rowm[:, s],
                                                                 rowm[:, s], ALU.mult), inc="dv")
                p.emit("vector", lambda e, s=sl: e.tensor_tensor(rowv[:, s], rowv[:, s],
                                                                 rowr[:, s], ALU.subtract), inc="dv")
                p.emit("scalar", lambda e, s=sl: e.activation(rowr[:, s], rowv[:, s], AF.Ln,
                                                              bias=epsc[:, 0:1], scale=1.0),
                       waits=[("dv", p.val("dv"))], inc="ac")
                p.emit("scalar", lambda e, s=sl: e.activation(rowr[:, s], rowr[:, s], AF.Exp,
                                                              scale=-0.5), inc="ac")
                p.emit("vector", lambda e, s=sl: e.tensor_copy(rowrb[:, s], rowr[:, s]),
                       waits=[("ac", p.val("ac"))], inc="dv")
                p.emit("vector", lambda e, s=sl: e.tensor_tensor(rowm[:, s], rowm[:, s],
                                                                 rowr[:, s], ALU.mult), inc="dv")
                p.emit("vector", lambda e, s=sl: e.tensor_copy(rowmb[:, s], rowm[:, s]), inc="dv")
                rb_done = ("dv", p.val("dv"))
                p.emit("tensor", lambda e, s=sl: e.matmul(qbc[0:128, 0:512], ones1b[:, :],
                                                          rowrb[:, s], start=True, stop=True),
                       waits=[rb_done], inc="pe")
                p.emit("tensor", lambda e, s=sl: e.matmul(qbc2[0:128, 0:512], ones1b[:, :],
                                                          rowmb[:, s], start=True, stop=True),
                       inc="pe")
                bc_done = ("pe", p.val("pe"))
                for kc in range(16):
                    xa = xbuf[:, kc * TQ + tt * 512: kc * TQ + tt * 512 + 512]
                    if store_to is not None:
                        p.emit("vector", lambda e, x=xa: e.tensor_tensor(xtile[:, :], x,
                                                                         qbc[0:128, 0:512],
                                                                         ALU.mult),
                               waits=[bc_done, ("st", p.val("st"))], inc="dv")
                        p.emit("vector", lambda e: e.tensor_tensor(xtile[:, :], xtile[:, :],
                                                                   qbc2[0:128, 0:512],
                                                                   ALU.subtract), inc="dv")
                        ST(store_to[kc * 128:(kc + 1) * 128, tt * 512:(tt + 1) * 512], xtile[:, :],
                           waits=[("dv", p.val("dv"))])
                    else:
                        p.emit("vector", lambda e, x=xa: e.tensor_tensor(x, x, qbc[0:128, 0:512],
                                                                         ALU.mult),
                               waits=[bc_done], inc="dv")
                        p.emit("vector", lambda e, x=xa: e.tensor_tensor(x, x, qbc2[0:128, 0:512],
                                                                         ALU.subtract), inc="dv")
            return ("dv", p.val("dv"))

        ln1_done = layernorm(x1p)
        if DBG_B == 2:
            DMA("gpsimd", dbgT[:, :].rearrange("(kc kp) t -> kp kc t", kp=128),
                x1p[:, :].rearrange("p (kc t) -> p kc t", kc=16), waits=[ln1_done])
        ST(x1d[:, :].rearrange("(kc kp) t -> kp kc t", kp=128), x1p[:, :].rearrange("p (kc t) -> p kc t", kc=16), waits=[ln1_done])
        x1_spill = ("st", p.val("st"))

        ev_g = ("ac", 0)
        ev_u = ("dv", 0)
        ev_fuse = ("dv", 0)
        ev_d = ("dv", 0)
        x1sl_free = ("pe", 0)
        for tq in range(4):
            DMA("gpsimd", x1sl[:, :].rearrange("p (kc t) -> p kc t", kc=16),
                x1d[:, tq * 512:(tq + 1) * 512].rearrange("(kc kp) t -> kp kc t", kp=128),
                waits=[x1_spill, x1sl_free, ev_d] + p.bar())
            ldx1 = p.val("ld")
            for ib in range(32):
                DMA("gpsimd", wgsl[:, :].rearrange("p (kc mm) -> p kc mm", kc=16),
                    wg8[:, ib * 256:(ib + 1) * 256].rearrange("(kc kp) mm -> kp kc mm", kp=128),
                    waits=p.bar())
                DMA("gpsimd", wusl[:, :].rearrange("p (kc mm) -> p kc mm", kc=16),
                    wu8[:, ib * 256:(ib + 1) * 256].rearrange("(kc kp) mm -> kp kc mm", kp=128))
                ldw = p.val("ld")
                for io in range(2):
                    ii = ib * 2 + io
                    for kc in range(16):
                        p.emit("tensor", lambda e, k=kc, o=io: e.matmul(
                            qb0[0:128, 0:512], wgsl[:, k * 256 + o * 128: k * 256 + o * 128 + 128],
                            x1sl[:, k * 512:(k + 1) * 512], start=(k == 0), stop=(k == 15)),
                            waits=([("ld", max(ldx1, ldw)), ev_g] +
                                   (p.bar() if io == 0 else [])) if kc == 0 else (),
                            inc="pe" if kc == 15 else None)
                    p.emit("scalar", lambda e, i2=ii: e.activation(gsb[:, :], qb0[0:128, 0:512],
                                                                   AF.Silu,
                                                                   scale=sgs[:, i2:i2 + 1]),
                           waits=[("pe", p.val("pe")), ev_fuse], inc="ac")
                    ev_g = ("ac", p.val("ac"))
                    for kc in range(16):
                        p.emit("tensor", lambda e, k=kc, o=io: e.matmul(
                            qb1[0:128, 0:512], wusl[:, k * 256 + o * 128: k * 256 + o * 128 + 128],
                            x1sl[:, k * 512:(k + 1) * 512], start=(k == 0), stop=(k == 15)),
                            waits=[ev_u] if kc == 0 else (), inc="pe" if kc == 15 else None)
                    p.emit("vector", lambda e, i2=ii: e.tensor_scalar(usb[:, :], qb1[0:128, 0:512],
                                                                      sus[:, i2:i2 + 1], None,
                                                                      ALU.mult),
                           waits=[("pe", p.val("pe")), ev_fuse], inc="dv")
                    ev_u = ("dv", p.val("dv"))
                    p.emit("vector", lambda e, i2=ii: e.tensor_tensor(
                        osb[:, i2 * 512:(i2 + 1) * 512], gsb[:, :], usb[:, :], ALU.mult),
                        waits=[ev_g], inc="dv")
                    ev_fuse = ("dv", p.val("dv"))
            fuse_done = ("dv", p.val("dv"))
            if DBG_B == 4 and tq == 0:
                DMA("gpsimd", dbgT[:, :].rearrange("(kc kp) t -> kp kc t", kp=128),
                    osb[:, :].rearrange("p (kc t) -> p kc t", kc=16), waits=[fuse_done])
            for m in range(16):
                for half in range(2):
                    DMA("gpsimd", wdsl0[:, :].rearrange("p (kc mm) -> p kc mm", kc=32),
                        wd8[4096 * half:4096 * (half + 1), m * 128:(m + 1) * 128].rearrange(
                            "(kc kp) mm -> kp kc mm", kp=128),
                        waits=p.bar())
                    ldwd = p.val("ld")
                    for kc in range(32):
                        kk = half * 32 + kc
                        p.emit("tensor", lambda e, k=kc, k2=kk, hf=half: e.matmul(
                            qbs[0:128, 0:512], wdsl0[:, k * 128:(k + 1) * 128],
                            osb[:, k2 * 512:(k2 + 1) * 512], start=(k2 == 0), stop=(k2 == 63)),
                            waits=[fuse_done, ("ld", ldwd), ev_d] if kc == 0 else (),
                            inc="pe" if kc == 31 else None)
                p.emit("vector", lambda e, mm=m, t=tq: e.scalar_tensor_tensor(
                    x1p[:, mm * TQ + t * 512: mm * TQ + t * 512 + 512],
                    qbs[0:128, 0:512], sds[:, mm:mm + 1],
                    x1sl[:, mm * 512:(mm + 1) * 512], ALU.mult, ALU.add),
                    waits=[("pe", p.val("pe")), ("st", p.val("st"))], inc="dv")
                ev_d = ("dv", p.val("dv"))
            x1sl_free = ("pe", p.val("pe"))
        phase_done = [ev_d]
        if DBG_B == 3:
            DMA("gpsimd", dbgT[:, :].rearrange("(kc kp) t -> kp kc t", kp=128),
                x1p[:, :].rearrange("p (kc t) -> p kc t", kc=16), waits=[ev_d])
        layernorm(x1p, store_to=outT)
        p.emit("gpsimd", lambda e: e.engine_nop(), waits=[("st", p.val("st"))], inc="po")
        _run_plan(nc, p, sems)
    es.close()
    return nc


def _consts():
    tt = np.arange(C)
    ii = np.arange(R)
    p_t = np.repeat(tt[None, :], R, 0).reshape(-1)
    p_i = np.repeat(ii[:, None], C, 1).reshape(-1)
    maskM = ((p_t[:, None] > p_t[None, :]) |
             ((p_t[:, None] == p_t[None, :]) & (p_i[:, None] > p_i[None, :]))).astype(np.float32)
    maskAT = (np.arange(C)[None, :] >= p_t[:, None]).astype(np.float32)
    return {
        "identc": np.eye(128, dtype=np.float32),
        "maskMc": maskM,
        "maskMTc": np.ascontiguousarray(maskM.T),
        "maskATc": maskAT,
        "onesr": np.ones((1, 128), np.float32),
        "onescl": np.ones((128, 1), np.float32),
    }


def _mixer_in_maps(inp, cst, xT):
    in_maps = []
    for c in range(8):
        b, hq = divmod(c, 4)
        hs = slice(4 * hq * DK, 4 * (hq + 1) * DK)
        hks = slice(4 * hq * R * DK, 4 * (hq + 1) * R * DK)
        hbs = slice(4 * hq * R, 4 * (hq + 1) * R)
        m = {
            "xT": xT[b],
            "wqT": np.ascontiguousarray(inp["L0_attn_Wq"][hs].T),
            "wkT": np.ascontiguousarray(inp["L0_attn_Wk"][hks].T),
            "wvT": np.ascontiguousarray(inp["L0_attn_Wv"][hks].T),
            "wbT": np.ascontiguousarray(inp["L0_attn_Wb"][hbs].T),
            "wf1T": np.ascontiguousarray(inp["L0_attn_Wf1"].T),
            "wg1T": np.ascontiguousarray(inp["L0_attn_Wg1"].T),
            "wf2T": np.ascontiguousarray(inp["L0_attn_Wf2"][hs].T),
            "wg2T": np.ascontiguousarray(inp["L0_attn_Wg2"][hs].T),
            "convq": np.ascontiguousarray(inp["L0_attn_conv_q"][hs]),
            "convk": np.ascontiguousarray(inp["L0_attn_conv_k"][hks]),
            "convv": np.ascontiguousarray(inp["L0_attn_conv_v"][hks]),
            "dtb": np.ascontiguousarray(inp["L0_attn_dt_bias"][hs, None]),
            "alogr": np.ascontiguousarray(inp["L0_attn_A_log"][None, 4 * hq:4 * hq + 4]),
            "bg1": np.ascontiguousarray(inp["L0_attn_bg1"][:, None]),
            "bg2": np.ascontiguousarray(inp["L0_attn_bg2"][hs, None]),
            "onormw": np.ascontiguousarray(inp["L0_attn_onorm_w"][:, None]),
        }
        m.update(cst)
        in_maps.append(m)
    return in_maps


def kernel(**inp):
    cst = _consts()
    x = np.ascontiguousarray(inp["L0_attn_hidden_states"])
    xT = [np.ascontiguousarray(x[b].T) for b in range(2)]

    nc_a = build_mixer()
    res_a = run_bass_kernel_spmd(nc_a, _mixer_in_maps(inp, cst, xT), core_ids=list(range(8)))
    og = [np.array(res_a.results[c]["ogated"], dtype=np.float32, copy=True) for c in range(8)]

    in_maps_b = []
    for c in range(8):
        b, tq = divmod(c, 4)
        ts = slice(tq * TQ, (tq + 1) * TQ)
        ofull = np.concatenate([og[4 * b + j][:, ts] for j in range(4)], axis=0)
        in_maps_b.append({
            "ofull": np.ascontiguousarray(ofull),
            "xTs": np.ascontiguousarray(xT[b][:, ts]),
            "woT": np.ascontiguousarray(inp["L0_attn_Wo"].T),
            "wg8": np.ascontiguousarray(inp["L0_ffn_w_gate"]),
            "wu8": np.ascontiguousarray(inp["L0_ffn_w_up"]),
            "wd8": np.ascontiguousarray(inp["L0_ffn_w_down"]),
            "sgate": np.ascontiguousarray(inp["L0_ffn_s_gate"][:, None]),
            "sup": np.ascontiguousarray(inp["L0_ffn_s_up"][:, None]),
            "sdown": np.ascontiguousarray(inp["L0_ffn_s_down"][:, None]),
            "onesr": cst["onesr"],
            "onescl": cst["onescl"],
        })

    nc_b = build_post()
    res_b = run_bass_kernel_spmd(nc_b, in_maps_b, core_ids=list(range(8)))

    out = np.empty((2, T, D), np.float32)
    for c in range(8):
        b, tq = divmod(c, 4)
        out[b, tq * TQ:(tq + 1) * TQ, :] = np.array(res_b.results[c]["outT"], copy=True).T
    return out


# version 11 (final)
# speedup vs baseline: 1.0661x; 1.0661x over previous
import contextlib
import numpy as np
import ml_dtypes
import concourse.bass as bass
import concourse.mybir as mybir
from concourse.bass_utils import run_bass_kernel_spmd

F32 = mybir.dt.float32
BF16 = mybir.dt.bfloat16
I8 = mybir.dt.int8
AF = mybir.ActivationFunctionType
ALU = mybir.AluOpType

T = 8192
D = 2048
DBG_B = 0
H = 16
HC = 4
R = 4
DK = 128
DV = 128
C = 32
BLK = 1024
NBLK = T // BLK
CPB = BLK // C
SCALE = DK ** -0.5
NORM_EPS = 1e-5
LN_EPS = 1e-5
TQ = T // 4


class Plan:

    def __init__(self):
        self.items = []
        self.cnt = {}
        self.seen = {}

    def val(self, sem):
        return self.cnt.get(sem, 0)

    def bar(self):
        return [(x, self.val(x)) for x in ("pe", "dv", "ac", "po", "ld", "st")]

    def emit(self, eng, fn, waits=(), inc=None, amt=1):
        fw = []
        seen = self.seen.setdefault(eng, {})
        for s, v in waits:
            if v > 0 and seen.get(s, 0) < v:
                fw.append((s, v))
                seen[s] = v
        if inc is not None:
            self.cnt[inc] = self.cnt.get(inc, 0) + amt
        self.items.append((eng, fn, tuple(fw), inc, amt))
        return (inc, self.cnt.get(inc, 0)) if inc is not None else None

    def body(self, eng_name, sems):
        items = self.items

        def body_fn(e):
            for en, fn, fw, inc, amt in items:
                if en != eng_name:
                    continue
                for s, v in fw:
                    e.wait_ge(sems[s], v)
                ins = fn(e)
                if inc is not None:
                    if ins is None:
                        ins = e.engine_nop()
                    ins.then_inc(sems[inc], amt)
        return body_fn


def _run_plan(nc, plan, sems):
    with nc.Block() as blk:
        blk.sync(plan.body("sync", sems))
        blk.gpsimd(plan.body("gpsimd", sems))
        blk.tensor(plan.body("tensor", sems))
        blk.vector(plan.body("vector", sems))
        blk.scalar(plan.body("scalar", sems))


def build_mixer():
    nc = bass.Bass("TRN2", target_bir_lowering=False)
    dt = nc.dram_tensor
    xT = dt("xT", [D, T], F32, kind="ExternalInput").ap()
    wqT = dt("wqT", [D, HC * DK], F32, kind="ExternalInput").ap()
    wkT = dt("wkT", [D, HC * R * DK], F32, kind="ExternalInput").ap()
    wvT = dt("wvT", [D, HC * R * DV], F32, kind="ExternalInput").ap()
    wbT = dt("wbT", [D, HC * R], F32, kind="ExternalInput").ap()
    wf1T = dt("wf1T", [D, DK], F32, kind="ExternalInput").ap()
    wg1T = dt("wg1T", [D, DV], F32, kind="ExternalInput").ap()
    wf2T = dt("wf2T", [DK, HC * DK], F32, kind="ExternalInput").ap()
    wg2T = dt("wg2T", [DV, HC * DV], F32, kind="ExternalInput").ap()
    convq = dt("convq", [HC * DK, 4], F32, kind="ExternalInput").ap()
    convk = dt("convk", [HC * R * DK, 4], F32, kind="ExternalInput").ap()
    convv = dt("convv", [HC * R * DV, 4], F32, kind="ExternalInput").ap()
    dtb = dt("dtb", [HC * DK, 1], F32, kind="ExternalInput").ap()
    alogr = dt("alogr", [1, HC], F32, kind="ExternalInput").ap()
    bg1 = dt("bg1", [DV, 1], F32, kind="ExternalInput").ap()
    bg2 = dt("bg2", [HC * DV, 1], F32, kind="ExternalInput").ap()
    onormw = dt("onormw", [DV, 1], F32, kind="ExternalInput").ap()
    identc = dt("identc", [128, 128], F32, kind="ExternalInput").ap()
    maskMc = dt("maskMc", [128, 128], F32, kind="ExternalInput").ap()
    maskMTc = dt("maskMTc", [128, 128], F32, kind="ExternalInput").ap()
    maskATc = dt("maskATc", [128, C], F32, kind="ExternalInput").ap()
    onesr = dt("onesr", [1, 128], F32, kind="ExternalInput").ap()
    onescl = dt("onescl", [128, 1], F32, kind="ExternalInput").ap()
    ogated = dt("ogated", [HC * DV, T], F32, kind="ExternalOutput").ap()
    betad = dt("betad", [HC * R, BLK], BF16).ap()

    NPLANE = HC * (1 + 2 * R)

    es = contextlib.ExitStack()
    xblk = es.enter_context(nc.sbuf_tensor("xblk", [128, 16 * BLK], BF16))
    wslab0 = es.enter_context(nc.sbuf_tensor("wslab0", [128, 16 * 128], BF16))
    wslab1 = es.enter_context(nc.sbuf_tensor("wslab1", [128, 16 * 128], BF16))
    wf2sb = es.enter_context(nc.sbuf_tensor("wf2sb", [128, HC * DK], BF16))
    wg2sb = es.enter_context(nc.sbuf_tensor("wg2sb", [128, HC * DV], BF16))
    raw0 = es.enter_context(nc.sbuf_tensor("raw0", [128, BLK + 3], BF16))
    raw1 = es.enter_context(nc.sbuf_tensor("raw1", [128, BLK + 3], BF16))
    halo = es.enter_context(nc.sbuf_tensor("halo", [128, 3 * NPLANE], BF16))
    qf = es.enter_context(nc.sbuf_tensor("qf", [128, HC * BLK], BF16))
    kf = es.enter_context(nc.sbuf_tensor("kf", [128, HC * R * BLK], BF16))
    vf = es.enter_context(nc.sbuf_tensor("vf", [128, HC * R * BLK], BF16))
    gate = es.enter_context(nc.sbuf_tensor("gate", [128, HC * BLK], BF16))
    Ghat = es.enter_context(nc.sbuf_tensor("Ghat", [128, HC * (BLK + 1)], F32))
    carry = es.enter_context(nc.sbuf_tensor("carry", [128, HC], F32))
    laS = es.enter_context(nc.sbuf_tensor("laS", [128, BLK], F32))
    sqS = es.enter_context(nc.sbuf_tensor("sqS", [128, BLK], BF16))
    f1sb = es.enter_context(nc.sbuf_tensor("f1sb", [128, BLK], BF16))
    g1sb = es.enter_context(nc.sbuf_tensor("g1sb", [128, BLK], BF16))
    betast = es.enter_context(nc.sbuf_tensor("betast", [HC * R, BLK], BF16))
    betafl = es.enter_context(nc.sbuf_tensor("betafl", [1, R * BLK], BF16))
    Awide = es.enter_context(nc.sbuf_tensor("Awide", [128, R * BLK], BF16))
    Bwide = es.enter_context(nc.sbuf_tensor("Bwide", [128, R * BLK], BF16))
    qhatw = es.enter_context(nc.sbuf_tensor("qhatw", [128, BLK], BF16))
    expCm = es.enter_context(nc.sbuf_tensor("expCm", [128, CPB], F32))
    cmat = es.enter_context(nc.sbuf_tensor("cmat", [128, CPB], F32))
    gammat = es.enter_context(nc.sbuf_tensor("gammat", [128, CPB], F32))
    anch = es.enter_context(nc.sbuf_tensor("anch", [128, CPB], F32))
    Bop = es.enter_context(nc.sbuf_tensor("Bop", [128, 128], BF16))
    Mm = es.enter_context(nc.sbuf_tensor("Mm", [128, 128], BF16))
    MTm = es.enter_context(nc.sbuf_tensor("MTm", [128, 128], BF16))
    Pk = es.enter_context(nc.sbuf_tensor("Pk", [128, 6 * 128], BF16))
    Qk = es.enter_context(nc.sbuf_tensor("Qk", [128, 5 * 128], BF16))
    Xt = es.enter_context(nc.sbuf_tensor("Xt", [128, 2 * 128], BF16))
    attnTm = es.enter_context(nc.sbuf_tensor("attnTm", [128, C], BF16))
    vmm = es.enter_context(nc.sbuf_tensor("vmm", [128, 128], BF16))
    wpf = es.enter_context(nc.sbuf_tensor("wpf", [128, 128], BF16))
    wpm = es.enter_context(nc.sbuf_tensor("wpm", [128, 128], BF16))
    Shat = es.enter_context(nc.sbuf_tensor("Shat", [128, 128], BF16))
    rhs_t = es.enter_context(nc.sbuf_tensor("rhs_t", [128, 128], BF16))
    ut = es.enter_context(nc.sbuf_tensor("ut", [128, 128], BF16))
    Sst = es.enter_context(nc.sbuf_tensor("Sst", [128, HC * 128], F32))
    ostage = es.enter_context(nc.sbuf_tensor("ostage", [128, HC * BLK], BF16))
    ogout = es.enter_context(nc.sbuf_tensor("ogout", [128, BLK], F32))
    rowf = es.enter_context(nc.sbuf_tensor("rowf", [1, 512], F32))
    rowb = es.enter_context(nc.sbuf_tensor("rowb", [1, 512], BF16))
    identb = es.enter_context(nc.sbuf_tensor("identb", [128, 128], BF16))
    maskMb = es.enter_context(nc.sbuf_tensor("maskMb", [128, 128], BF16))
    maskMTb = es.enter_context(nc.sbuf_tensor("maskMTb", [128, 128], BF16))
    maskATb = es.enter_context(nc.sbuf_tensor("maskATb", [128, C], BF16))
    ones1b = es.enter_context(nc.sbuf_tensor("ones1b", [1, 128], BF16))
    ones1f = es.enter_context(nc.sbuf_tensor("ones1f", [1, 128], F32))
    onescb = es.enter_context(nc.sbuf_tensor("onescb", [128, 1], BF16))
    convq_s = es.enter_context(nc.sbuf_tensor("convq_s", [128, 4 * HC], F32))
    convk_s = es.enter_context(nc.sbuf_tensor("convk_s", [128, 4 * HC * R], F32))
    convv_s = es.enter_context(nc.sbuf_tensor("convv_s", [128, 4 * HC * R], F32))
    dtb_s = es.enter_context(nc.sbuf_tensor("dtb_s", [128, HC], F32))
    bg1_s = es.enter_context(nc.sbuf_tensor("bg1_s", [128, 1], F32))
    bg2_s = es.enter_context(nc.sbuf_tensor("bg2_s", [128, HC], F32))
    onorm_s = es.enter_context(nc.sbuf_tensor("onorm_s", [128, 1], F32))
    onecol = es.enter_context(nc.sbuf_tensor("onecol", [128, 1], F32))
    epscol = es.enter_context(nc.sbuf_tensor("epscol", [1, 1], F32))
    alog_s = es.enter_context(nc.sbuf_tensor("alog_s", [1, HC], F32))
    nega_row = es.enter_context(nc.sbuf_tensor("nega_row", [1, HC], F32))
    nega_b = es.enter_context(nc.sbuf_tensor("nega_b", [128, HC], F32))
    zcol = es.enter_context(nc.sbuf_tensor("zcol", [128, 1], F32))
    pb0 = es.enter_context(nc.psum_tensor("pb0", [128, 512], F32))
    pb1 = es.enter_context(nc.psum_tensor("pb1", [128, 512], F32))
    pbc = es.enter_context(nc.psum_tensor("pbc", [128, 512], F32))
    pbm = es.enter_context(nc.psum_tensor("pbm", [128, 512], F32))
    pbs0 = es.enter_context(nc.psum_tensor("pbs0", [128, 512], F32))
    pbs1 = es.enter_context(nc.psum_tensor("pbs1", [128, 512], F32))
    pbl = es.enter_context(nc.psum_tensor("pbl", [128, 512], F32))
    pbt = es.enter_context(nc.psum_tensor("pbt", [128, 256], BF16))
    s_ld = es.enter_context(nc.semaphore("ld"))
    s_pe = es.enter_context(nc.semaphore("pe"))
    s_dv = es.enter_context(nc.semaphore("dv"))
    s_ac = es.enter_context(nc.semaphore("ac"))
    s_po = es.enter_context(nc.semaphore("po"))
    s_st = es.enter_context(nc.semaphore("st"))
    if True:
        sems = {"ld": s_ld, "pe": s_pe, "dv": s_dv, "ac": s_ac, "po": s_po, "st": s_st}
        p = Plan()

        def DMA(eng, out, in_, waits=(), slow=False):
            return p.emit(eng, lambda e, o=out, i=in_, sl=slow: e.dma_start(
                out=o, in_=i, allow_slow_non_contiguous=sl),
                waits=waits, inc="ld", amt=16)

        def ST(out, in_, waits=()):
            return p.emit("sync", lambda e, o=out, i=in_: e.dma_start(out=o, in_=i),
                          waits=waits, inc="st", amt=16)

        for dst, src in [
            (identb[:, :], identc[:, :]), (maskMb[:, :], maskMc[:, :]),
            (maskMTb[:, :], maskMTc[:, :]), (maskATb[:, :], maskATc[:, :]),
            (ones1b[:, :], onesr[:, :]), (onescb[:, :], onescl[:, :]),
            (ones1f[:, :], onesr[:, :]),
            (convq_s[:, :].rearrange("p (m j) -> p m j", j=4), convq[:, :].rearrange("(m p) j -> p m j", p=128)),
            (convk_s[:, :].rearrange("p (m j) -> p m j", j=4), convk[:, :].rearrange("(m p) j -> p m j", p=128)),
            (convv_s[:, :].rearrange("p (m j) -> p m j", j=4), convv[:, :].rearrange("(m p) j -> p m j", p=128)),
            (dtb_s[:, :].unsqueeze(2), dtb[:, :].rearrange("(m p) one -> p m one", p=128)),
            (bg1_s[:, :], bg1[:, :]),
            (bg2_s[:, :].unsqueeze(2), bg2[:, :].rearrange("(m p) one -> p m one", p=128)),
            (onorm_s[:, :], onormw[:, :]),
            (alog_s[:, :], alogr[:, :]),
            (wf2sb[:, :], wf2T[:, :]), (wg2sb[:, :], wg2T[:, :]),
        ]:
            DMA("gpsimd", dst, src, slow=True)
        ld_pre = p.val("ld")
        p.emit("vector", lambda e: e.memset(onecol[:, :], 1.0), inc="dv")
        p.emit("vector", lambda e: e.memset(epscol[:, :], NORM_EPS), inc="dv")
        p.emit("vector", lambda e: e.memset(zcol[:, :], 0.0), inc="dv")
        p.emit("vector", lambda e: e.memset(carry[:, :], 0.0), inc="dv")
        p.emit("vector", lambda e: e.memset(Sst[:, :], 0.0), inc="dv")
        p.emit("vector", lambda e: e.memset(halo[:, :], 0.0), inc="dv")
        p.emit("scalar", lambda e: e.activation(nega_row[:, :], alog_s[:, :], AF.Exp),
               waits=[("ld", ld_pre)], inc="ac")
        p.emit("vector", lambda e: e.tensor_scalar(nega_row[:, :], nega_row[:, :], -1.0, None, ALU.mult),
               waits=[("ac", p.val("ac"))], inc="dv")
        p.emit("tensor", lambda e: e.matmul(pbc[0:128, 0:HC], ones1f[:, :], nega_row[:, :],
                                            start=True, stop=True),
               waits=[("dv", p.val("dv")), ("ld", ld_pre)], inc="pe")
        p.emit("vector", lambda e: e.tensor_copy(nega_b[:, :], pbc[0:128, 0:HC]),
               waits=[("pe", p.val("pe"))], inc="dv")
        pre_dv = p.val("dv")

        planes = []
        for m in range(4):
            planes.append(("q", qf, m * BLK, convq_s, 4 * m))
        for m in range(16):
            planes.append(("k", kf, m * BLK, convk_s, 4 * m))
        for m in range(16):
            planes.append(("v", vf, m * BLK, convv_s, 4 * m))

        def wsrc(m):
            if m < 4:
                return wqT[:, m * 128:(m + 1) * 128], 128
            if m < 20:
                return wkT[:, (m - 4) * 128:(m - 3) * 128], 128
            if m < 36:
                return wvT[:, (m - 20) * 128:(m - 19) * 128], 128
            if m == 36:
                return wbT[:, :], 16
            if m == 37:
                return wf1T[:, :], 128
            return wg1T[:, :], 128

        NM = 39
        slabs = [wslab0, wslab1]
        slab_free = [("pe", 0), ("pe", 0)]
        conv_done = {}

        for blk_i in range(NBLK):
            t0 = blk_i * BLK
            xsrc = xT[:, t0:t0 + BLK].rearrange("(kc kp) t -> kp kc t", kp=128)
            DMA("gpsimd", xblk[:, :].rearrange("p (kc t) -> p kc t", kc=16), xsrc, waits=[("pe", p.val("pe"))])
            ld_x = p.val("ld")

            src0, w0 = wsrc(0)
            DMA("gpsimd", slabs[0][:, 0:16 * w0].rearrange("p (kc mm) -> p kc mm", kc=16),
                src0.rearrange("(kc kp) mm -> kp kc mm", kp=128), waits=[slab_free[0]])
            ld_slab = [p.val("ld"), 0]

            raw_done = {}
            ps_free = {0: ("dv", 0), 1: ("dv", 0)}
            raw_free = {0: ("dv", 0), 1: ("dv", 0)}
            for m in range(NM):
                cur = m % 2
                width = wsrc(m)[1]
                if m + 1 < NM:
                    srcn, wn = wsrc(m + 1)
                    DMA("gpsimd", slabs[1 - cur][:, 0:16 * wn].rearrange("p (kc mm) -> p kc mm", kc=16),
                        srcn.rearrange("(kc kp) mm -> kp kc mm", kp=128),
                        waits=[slab_free[1 - cur]])
                    ld_slab[1 - cur] = p.val("ld")
                slab = slabs[cur]
                is_plane = m < 36
                raw = (raw0, raw1)[m % 2] if is_plane else None
                for th in range(2):
                    ps = (pb0, pb1)[th]
                    mmw = [("ld", max(ld_x, ld_slab[cur])), ps_free[th]]
                    if th == 0:
                        mmw += p.bar()
                    if is_plane:
                        mmw.append(raw_free[m % 2])
                    for kc in range(16):
                        w_ap = slab[:, kc * width:(kc + 1) * width]
                        x_ap = xblk[:, kc * BLK + th * 512: kc * BLK + th * 512 + 512]
                        p.emit("tensor",
                               lambda e, o=ps[0:width, 0:512], w=w_ap, x=x_ap,
                               a=(kc == 0), b=(kc == 15): e.matmul(o, w, x, start=a, stop=b),
                               waits=mmw if kc == 0 else (), inc="pe" if kc == 15 else None)
                    pe_mm = p.val("pe")
                    if is_plane:
                        dst = raw[:, 3 + th * 512: 3 + th * 512 + 512]
                        if (m + th) % 2 == 0:
                            p.emit("vector", lambda e, d=dst, s=ps: e.tensor_copy(d, s[0:128, 0:512]),
                                   waits=[("pe", pe_mm), raw_free[m % 2]], inc="dv")
                            raw_done[(m, th)] = ("dv", p.val("dv"))
                            ps_free[th] = ("dv", p.val("dv"))
                        else:
                            p.emit("scalar", lambda e, d=dst, s=ps: e.copy(d, s[0:128, 0:512]),
                                   waits=[("pe", pe_mm), raw_free[m % 2]], inc="ac")
                            raw_done[(m, th)] = ("ac", p.val("ac"))
                            ps_free[th] = ("ac", p.val("ac"))
                    elif m == 36:
                        p.emit("scalar", lambda e, d=betast[:, th * 512: th * 512 + 512], s=ps:
                               e.activation(d, s[0:16, 0:512], AF.Sigmoid),
                               waits=[("pe", pe_mm), ("st", p.val("st"))], inc="ac")
                        raw_done[(m, th)] = ("ac", p.val("ac"))
                        ps_free[th] = ("ac", p.val("ac"))
                    elif m == 37:
                        p.emit("vector", lambda e, d=f1sb[:, th * 512: th * 512 + 512], s=ps:
                               e.tensor_copy(d, s[0:128, 0:512]),
                               waits=[("pe", pe_mm)], inc="dv")
                        raw_done[(m, th)] = ("dv", p.val("dv"))
                        ps_free[th] = ("dv", p.val("dv"))
                    else:
                        p.emit("vector", lambda e, d=g1sb[:, th * 512: th * 512 + 512], s=ps:
                               e.tensor_scalar(d, s[0:128, 0:512], bg1_s[:, 0:1], None, ALU.add),
                               waits=[("pe", pe_mm)], inc="dv")
                        raw_done[(m, th)] = ("dv", p.val("dv"))
                        ps_free[th] = ("dv", p.val("dv"))
                slab_free[cur] = ("pe", p.val("pe"))
                if is_plane:
                    kind, fin, foff, cw, cwoff = planes[m]
                    both = [raw_done[(m, 0)], raw_done[(m, 1)]]
                    p.emit("vector", lambda e, r=raw, pl=m: e.tensor_copy(r[:, 0:3], halo[:, 3 * pl:3 * pl + 3]),
                           waits=both, inc="dv")
                    p.emit("vector", lambda e, r=raw, pl=m: e.tensor_copy(halo[:, 3 * pl:3 * pl + 3],
                                                                          r[:, BLK:BLK + 3]),
                           waits=both, inc="dv")
                    hv = ("dv", p.val("dv"))
                    if kind == "q":
                        dst = fin[:, foff:foff + BLK]
                        dst3 = dst.rearrange("p (n t) -> p n t", n=CPB)
                    else:
                        hh2, ii2 = divmod(m - (4 if kind == "k" else 20), 4)
                        base = hh2 * (R * BLK) + ii2 * C
                        dst3 = fin[:, base:base + R * BLK - ii2 * C if False else None]
                        dst3 = fin[:, :].rearrange("p (hh n i t) -> p hh n i t", hh=HC, n=CPB, i=R)[
                            :, hh2, :, ii2, :]
                        dst = dst3
                    eng, sem = ("vector", "dv")
                    p.emit(eng, lambda e, d=dst3, r=raw, w=cw[:, cwoff:cwoff + 1]:
                           e.tensor_scalar(d, r[:, 0:BLK].rearrange("p (n t) -> p n t", n=CPB),
                                           w, None, ALU.mult),
                           waits=[hv, ("ld", ld_pre), ("pe", p.val("pe"))], inc=sem)
                    for j in range(1, 4):
                        p.emit(eng, lambda e, d=dst3, r=raw, w=cw[:, cwoff + j:cwoff + j + 1], jj=j:
                               e.scalar_tensor_tensor(
                                   d, r[:, jj:jj + BLK].rearrange("p (n t) -> p n t", n=CPB),
                                   w, d, ALU.mult, ALU.add),
                               inc=sem)
                    cv = (sem, p.val(sem))
                    raw_free[m % 2] = cv
                    p.emit("scalar", lambda e, d=dst3: e.activation(d, d, AF.Silu),
                           waits=[cv], inc="ac")
                    conv_done[m] = ("ac", p.val("ac"))

            for pl in range(20):
                kind, fin, foff, _, _ = planes[pl]
                if kind == "q":
                    dst = fin[:, foff:foff + BLK]
                else:
                    hh2, ii2 = divmod(pl - 4, 4)
                    dst = fin[:, :].rearrange("p (hh n i t) -> p hh n i t", hh=HC, n=CPB, i=R)[
                        :, hh2, :, ii2, :]
                p.emit("scalar", lambda e, d=dst: e.activation(
                    sqS[:, :].rearrange("p (n t) -> p n t", n=CPB) if len(d.shape) == 3 else sqS[:, :],
                    d, AF.Square),
                       waits=[conv_done[pl]] + p.bar(), inc="ac")
                sq_done = ("ac", p.val("ac"))
                for th in range(2):
                    p.emit("tensor", lambda e, t=th: e.matmul(pbs0[0:1, 0:512], onescb[:, :],
                                                              sqS[:, t * 512:t * 512 + 512],
                                                              start=True, stop=True),
                           waits=[sq_done, ("ac", p.val("ac")), ("dv", p.val("dv"))], inc="pe")
                    mmv = ("pe", p.val("pe"))
                    p.emit("vector", lambda e: e.tensor_scalar(rowf[:, 0:512], pbs0[0:1, 0:512],
                                                               1e-24, None, ALU.max),
                           waits=[mmv], inc="dv")
                    p.emit("scalar", lambda e: e.activation(rowf[:, 0:512], rowf[:, 0:512], AF.Ln),
                           waits=[("dv", p.val("dv"))], inc="ac")
                    p.emit("scalar", lambda e: e.activation(rowb[:, 0:512], rowf[:, 0:512], AF.Exp,
                                                            scale=-0.5), inc="ac")
                    rr = ("ac", p.val("ac"))
                    p.emit("tensor", lambda e: e.matmul(pbc[0:128, 0:512], ones1b[:, :],
                                                        rowb[:, 0:512], start=True, stop=True),
                           waits=[rr], inc="pe")
                    if len(dst.shape) == 3:
                        dsl = dst[:, th * 16:(th + 1) * 16, :]
                        psl = pbc[0:128, 0:512].rearrange("p (n t) -> p n t", n=16)
                    else:
                        dsl = dst[:, th * 512:th * 512 + 512]
                        psl = pbc[0:128, 0:512]
                    p.emit("vector", lambda e, d=dsl, s2=psl: e.tensor_tensor(d, d, s2, ALU.mult),
                           waits=[("pe", p.val("pe"))], inc="dv")
                conv_done[pl] = ("dv", p.val("dv"))

            ST(betad[:, :], betast[:, :], waits=[raw_done[(36, 0)], raw_done[(36, 1)]])
            beta_st = ("st", p.val("st"))

            fg_done = [raw_done[(37, 0)], raw_done[(37, 1)], raw_done[(38, 0)], raw_done[(38, 1)]]
            for h in range(HC):
                for th in range(2):
                    p.emit("tensor", lambda e, hh=h, t=th: e.matmul(
                        pbm[0:128, 0:512], wf2sb[:, hh * DK:(hh + 1) * DK],
                        f1sb[:, t * 512:t * 512 + 512], start=True, stop=True),
                        waits=fg_done + [("ld", ld_pre)] + p.bar(),
                        inc="pe")
                    mmv = ("pe", p.val("pe"))
                    p.emit("scalar", lambda e, hh=h, t=th: e.activation(
                        laS[:, t * 512:t * 512 + 512], pbm[0:128, 0:512], AF.Exp,
                        bias=dtb_s[:, hh:hh + 1], scale=1.0), waits=[mmv], inc="ac")
                    p.emit("scalar", lambda e, t=th: e.activation(
                        laS[:, t * 512:t * 512 + 512], laS[:, t * 512:t * 512 + 512], AF.Ln,
                        bias=onecol[:, 0:1], scale=1.0), inc="ac")
                    p.emit("vector", lambda e, hh=h, t=th: e.tensor_scalar(
                        laS[:, t * 512:t * 512 + 512], laS[:, t * 512:t * 512 + 512],
                        nega_b[:, hh:hh + 1], None, ALU.mult),
                        waits=[("ac", p.val("ac")), ("dv", pre_dv)], inc="dv")
                gh = Ghat[:, h * (BLK + 1):(h + 1) * (BLK + 1)]
                p.emit("vector", lambda e, g=gh, hh=h: e.tensor_copy(g[:, 0:1], carry[:, hh:hh + 1]),
                       inc="dv")
                p.emit("vector", lambda e, g=gh, hh=h: e.tensor_tensor_scan(
                    g[:, 1:BLK + 1], laS[:, :], zcol[:, 0:1].broadcast_to([128, BLK]),
                    carry[:, hh:hh + 1], ALU.add, ALU.add), inc="dv")
                p.emit("vector", lambda e, g=gh, hh=h: e.tensor_copy(carry[:, hh:hh + 1],
                                                                     g[:, BLK:BLK + 1]), inc="dv")
                for th in range(2):
                    p.emit("tensor", lambda e, hh=h, t=th: e.matmul(
                        pbm[0:128, 0:512], wg2sb[:, hh * DV:(hh + 1) * DV],
                        g1sb[:, t * 512:t * 512 + 512], start=True, stop=True),
                        waits=[("ac", p.val("ac")), ("dv", p.val("dv"))], inc="pe")
                    p.emit("scalar", lambda e, hh=h, t=th: e.activation(
                        gate[:, hh * BLK + t * 512: hh * BLK + t * 512 + 512],
                        pbm[0:128, 0:512], AF.Sigmoid, bias=bg2_s[:, hh:hh + 1], scale=1.0),
                        waits=[("pe", p.val("pe"))], inc="ac")

            for h in range(HC):
                DMA("gpsimd", betafl[:, :].rearrange("one (n i t) -> one n i t", n=CPB, i=R),
                    betad[h * R:(h + 1) * R, :].rearrange("i (n t) -> n i t", n=CPB).unsqueeze(0),
                    waits=[beta_st, ("pe", p.val("pe"))], slow=True)
                ld_beta = p.val("ld")
                gh = Ghat[:, h * (BLK + 1):(h + 1) * (BLK + 1)]
                gall = gh[:, 1:BLK + 1].rearrange("p (c t) -> p c t", c=CPB)
                miA = gh[:, C // 2::C]
                pA = gh[:, 0::C]
                eA = gh[:, C::C]
                scan_pre_w = p.bar()
                p.emit("vector", lambda e, g=gall, mi=miA: e.tensor_tensor(
                    laS[:, :].rearrange("p (c t) -> p c t", c=CPB), g,
                    mi[:, 0:CPB].unsqueeze(2).broadcast_to([128, CPB, C]), ALU.subtract),
                    waits=scan_pre_w, inc="dv")
                p.emit("scalar", lambda e: e.activation(sqS[:, :], laS[:, :], AF.Exp),
                       waits=[("dv", p.val("dv"))], inc="ac")
                exA = ("ac", p.val("ac"))
                p.emit("vector", lambda e, hh=h: e.tensor_tensor(
                    qhatw[:, :], qf[:, hh * BLK:(hh + 1) * BLK], sqS[:, :], ALU.mult),
                    waits=[exA], inc="dv")
                for i2 in range(R):
                    p.emit("vector", lambda e, hh=h, i3=i2: e.tensor_tensor(
                        Awide[:, :].rearrange("p (n i t) -> p n i t", n=CPB, i=R)[:, :, i3, :],
                        kf[:, :].rearrange("p (hh n i t) -> p hh n i t", hh=HC, n=CPB, i=R)[:, hh, :, i3, :],
                        sqS[:, :].rearrange("p (n t) -> p n t", n=CPB), ALU.mult), inc="dv")
                aw_done = ("dv", p.val("dv"))
                p.emit("vector", lambda e, g=gall, mi=miA: e.tensor_tensor(
                    laS[:, :].rearrange("p (c t) -> p c t", c=CPB),
                    mi[:, 0:CPB].unsqueeze(2).broadcast_to([128, CPB, C]), g, ALU.subtract),
                    waits=[aw_done], inc="dv")
                p.emit("scalar", lambda e: e.activation(sqS[:, :], laS[:, :], AF.Exp),
                       waits=[("dv", p.val("dv"))], inc="ac")
                exB = ("ac", p.val("ac"))
                for i2 in range(R):
                    p.emit("vector", lambda e, hh=h, i3=i2: e.tensor_tensor(
                        Bwide[:, :].rearrange("p (n i t) -> p n i t", n=CPB, i=R)[:, :, i3, :],
                        kf[:, :].rearrange("p (hh n i t) -> p hh n i t", hh=HC, n=CPB, i=R)[:, hh, :, i3, :],
                        sqS[:, :].rearrange("p (n t) -> p n t", n=CPB), ALU.mult),
                        waits=[exB] if i2 == 0 else (), inc="dv")
                p.emit("vector", lambda e, miv=miA, ev=eA: e.tensor_tensor(
                    anch[:, :], ev[:, 0:CPB], miv[:, 0:CPB], ALU.subtract), inc="dv")
                p.emit("scalar", lambda e: e.activation(expCm[:, :], anch[:, :], AF.Exp),
                       waits=[("dv", p.val("dv"))], inc="ac")
                p.emit("vector", lambda e, miv=miA, pv=pA: e.tensor_tensor(
                    anch[:, :], miv[:, 0:CPB], pv[:, 0:CPB], ALU.subtract),
                    waits=[("ac", p.val("ac"))], inc="dv")
                p.emit("scalar", lambda e: e.activation(cmat[:, :], anch[:, :], AF.Exp),
                       waits=[("dv", p.val("dv"))], inc="ac")
                p.emit("vector", lambda e, ev=eA, pv=pA: e.tensor_tensor(
                    anch[:, :], ev[:, 0:CPB], pv[:, 0:CPB], ALU.subtract),
                    waits=[("ac", p.val("ac"))], inc="dv")
                p.emit("scalar", lambda e: e.activation(gammat[:, :], anch[:, :], AF.Exp),
                       waits=[("dv", p.val("dv"))], inc="ac")
                pre_done = [("ac", p.val("ac")), ("dv", p.val("dv"))]

                for n in range(CPB):
                    A3 = Awide[:, n * 128:(n + 1) * 128]
                    B3 = Bwide[:, n * 128:(n + 1) * 128]
                    V3 = vf[:, h * R * BLK + n * 128: h * R * BLK + (n + 1) * 128]
                    qsl = qhatw[:, n * C:(n + 1) * C]
                    bsl = betafl[:, n * 128:(n + 1) * 128]
                    p.emit("tensor", lambda e, b=bsl: e.matmul(pbc[0:128, 0:128], ones1b[:, :], b,
                                                               start=True, stop=True),
                           waits=(pre_done if n == 0 else []) + [("ld", ld_beta)] + p.bar(),
                           inc="pe")
                    p.emit("vector", lambda e, b=B3: e.tensor_tensor(
                        Bop[:, :], b, pbc[0:128, 0:128], ALU.mult),
                        waits=[("pe", p.val("pe"))], inc="dv")
                    bop_done = ("dv", p.val("dv"))
                    p.emit("tensor", lambda e, a=A3: e.matmul(pb0[0:128, 0:128], a, Bop[:, :],
                                                              start=True, stop=True),
                           waits=[bop_done], inc="pe")
                    p.emit("vector", lambda e: e.tensor_tensor(Mm[:, :], pb0[0:128, 0:128],
                                                               maskMb[:, :], ALU.mult),
                           waits=[("pe", p.val("pe"))], inc="dv")
                    p.emit("tensor", lambda e, a=A3: e.matmul(pb1[0:128, 0:128], Bop[:, :], a,
                                                              start=True, stop=True), inc="pe")
                    p.emit("vector", lambda e: e.tensor_tensor(MTm[:, :], pb1[0:128, 0:128],
                                                               maskMTb[:, :], ALU.mult),
                           waits=[("pe", p.val("pe"))], inc="dv")
                    m_done = ("dv", p.val("dv"))
                    p.emit("gpsimd", lambda e: e.tensor_tensor(Xt[:, 0:128], identb[:, :],
                                                               MTm[:, :], ALU.subtract),
                           waits=[m_done, ("pe", p.val("pe"))], inc="po")
                    x_cur = 0
                    x_done = ("po", p.val("po"))
                    prevP_ap, prevQ_ap = Mm[:, :], MTm[:, :]
                    for k in range(1, 7):
                        pk = Pk[:, (k - 1) * 128:k * 128]
                        p.emit("tensor", lambda e, l=prevQ_ap, r2=prevP_ap: e.matmul(
                            pbs0[0:128, 0:128], l, r2, start=True, stop=True),
                            waits=[m_done] if k == 1 else [("ac", p.val("ac"))], inc="pe")
                        p.emit("scalar", lambda e, d=pk: e.copy(d, pbs0[0:128, 0:128]),
                               waits=[("pe", p.val("pe"))], inc="ac")
                        pk_done = ("ac", p.val("ac"))
                        if k < 6:
                            qk = Qk[:, (k - 1) * 128:k * 128]
                            p.emit("tensor", lambda e, l=prevP_ap, r2=prevQ_ap: e.matmul(
                                pbs1[0:128, 0:128], l, r2, start=True, stop=True), inc="pe")
                            p.emit("scalar", lambda e, d=qk: e.copy(d, pbs1[0:128, 0:128]),
                                   waits=[("pe", p.val("pe"))], inc="ac")
                            prevQ_ap = qk
                        p.emit("tensor", lambda e, l=pk, x=Xt[:, x_cur * 128:(x_cur + 1) * 128]:
                               e.matmul(pbl[0:128, 0:128], l, x, start=True, stop=True),
                               waits=[pk_done, x_done], inc="pe")
                        p.emit("vector", lambda e, xo=Xt[:, x_cur * 128:(x_cur + 1) * 128],
                               xn=Xt[:, (1 - x_cur) * 128:(2 - x_cur) * 128]:
                               e.tensor_tensor(xn, xo, pbl[0:128, 0:128], ALU.add),
                               waits=[("pe", p.val("pe"))], inc="dv")
                        x_cur = 1 - x_cur
                        x_done = ("dv", p.val("dv"))
                        prevP_ap = pk
                    TT_ap = Xt[:, x_cur * 128:(x_cur + 1) * 128]
                    p.emit("tensor", lambda e, q=qsl: e.matmul(pb0[0:128, 0:C], Bop[:, :], q,
                                                               start=True, stop=True),
                           waits=[("dv", p.val("dv"))], inc="pe")
                    p.emit("vector", lambda e: e.tensor_tensor(attnTm[:, :], pb0[0:128, 0:C],
                                                               maskATb[:, :], ALU.mult),
                           waits=[("pe", p.val("pe"))], inc="dv")
                    at_done = ("dv", p.val("dv"))
                    p.emit("tensor", lambda e, v=V3: e.transpose(pbt[0:128, 0:128], v, identb[:, :]),
                           waits=[at_done], inc="pe")
                    p.emit("scalar", lambda e: e.copy(vmm[:, :], pbt[0:128, 0:128]),
                           waits=[("pe", p.val("pe")), ("dv", p.val("dv"))], inc="ac")
                    vmm_done = ("ac", p.val("ac"))
                    p.emit("vector", lambda e, nn=n: e.tensor_scalar(wpf[:, :], Bop[:, :],
                                                                     expCm[:, nn:nn + 1], None,
                                                                     ALU.mult),
                           waits=[bop_done], inc="dv")
                    p.emit("tensor", lambda e: e.transpose(pbt[0:128, 128:256], wpf[:, :],
                                                           identb[:, :]),
                           waits=[("dv", p.val("dv")), vmm_done], inc="pe")
                    p.emit("scalar", lambda e: e.copy(wpm[:, :], pbt[0:128, 128:256]),
                           waits=[("pe", p.val("pe"))], inc="ac")
                    wpm_done = ("ac", p.val("ac"))
                    Sh = Sst[:, h * 128:(h + 1) * 128]
                    p.emit("vector", lambda e, s=Sh, nn=n: e.tensor_scalar(
                        Shat[:, :], s, cmat[:, nn:nn + 1], None, ALU.mult),
                        waits=[("pe", p.val("pe"))], inc="dv")
                    sh_done = ("dv", p.val("dv"))
                    p.emit("tensor", lambda e, a=A3: e.matmul(pb1[0:128, 0:128], a, Shat[:, :],
                                                              start=True, stop=True),
                           waits=[sh_done], inc="pe")
                    p.emit("vector", lambda e: e.tensor_tensor(rhs_t[:, :], vmm[:, :],
                                                               pb1[0:128, 0:128], ALU.subtract),
                           waits=[("pe", p.val("pe")), vmm_done], inc="dv")
                    p.emit("tensor", lambda e, x=TT_ap: e.matmul(pbl[0:128, 0:128], x, rhs_t[:, :],
                                                                 start=True, stop=True),
                           waits=[("dv", p.val("dv"))], inc="pe")
                    p.emit("vector", lambda e: e.tensor_copy(ut[:, :], pbl[0:128, 0:128]),
                           waits=[("pe", p.val("pe"))], inc="dv")
                    ut_done = ("dv", p.val("dv"))
                    p.emit("tensor", lambda e, q=qsl: e.matmul(pb0[0:128, 0:C], Shat[:, :], q,
                                                               start=True, stop=False),
                           waits=[ut_done])
                    p.emit("tensor", lambda e: e.matmul(pb0[0:128, 0:C], ut[:, :], attnTm[:, :],
                                                        start=False, stop=True), inc="pe")
                    p.emit("vector", lambda e, hh=h, nn=n: e.tensor_scalar(
                        ostage[:, hh * BLK + nn * C: hh * BLK + (nn + 1) * C],
                        pb0[0:128, 0:C], SCALE, None, ALU.mult),
                        waits=[("pe", p.val("pe"))], inc="dv")
                    p.emit("tensor", lambda e: e.matmul(pbm[0:128, 0:128], wpm[:, :], ut[:, :],
                                                        start=True, stop=True),
                           waits=[wpm_done, ut_done], inc="pe")
                    p.emit("vector", lambda e, s=Sh, nn=n: e.scalar_tensor_tensor(
                        s, s, gammat[:, nn:nn + 1], pbm[0:128, 0:128], ALU.mult, ALU.add),
                        waits=[("pe", p.val("pe"))], inc="dv")

                oh = ostage[:, h * BLK:(h + 1) * BLK]
                p.emit("scalar", lambda e, o=oh: e.activation(sqS[:, :], o, AF.Square),
                       waits=p.bar(), inc="ac")
                sq_done = ("ac", p.val("ac"))
                for th in range(2):
                    p.emit("tensor", lambda e, t=th: e.matmul(pbs0[0:1, 0:512], onescb[:, :],
                                                              sqS[:, t * 512:t * 512 + 512],
                                                              start=True, stop=True),
                           waits=[sq_done, ("ac", p.val("ac")), ("dv", p.val("dv"))], inc="pe")
                    p.emit("scalar", lambda e: e.activation(rowf[:, 0:512], pbs0[0:1, 0:512],
                                                            AF.Ln, bias=epscol[:, 0:1],
                                                            scale=1.0 / DV),
                           waits=[("pe", p.val("pe"))], inc="ac")
                    p.emit("scalar", lambda e: e.activation(rowb[:, 0:512], rowf[:, 0:512],
                                                            AF.Exp, scale=-0.5), inc="ac")
                    rr = ("ac", p.val("ac"))
                    p.emit("tensor", lambda e: e.matmul(pbc[0:128, 0:512], ones1b[:, :],
                                                        rowb[:, 0:512], start=True, stop=True),
                           waits=[rr], inc="pe")
                    p.emit("vector", lambda e, o=oh, t=th: e.tensor_tensor(
                        sqS[:, t * 512:t * 512 + 512], o[:, t * 512:t * 512 + 512],
                        pbc[0:128, 0:512], ALU.mult),
                        waits=[("pe", p.val("pe"))], inc="dv")
                    p.emit("vector", lambda e, hh=h, t=th: e.tensor_tensor(
                        sqS[:, t * 512:t * 512 + 512], sqS[:, t * 512:t * 512 + 512],
                        gate[:, hh * BLK + t * 512: hh * BLK + t * 512 + 512], ALU.mult), inc="dv")
                    p.emit("vector", lambda e, t=th: e.tensor_scalar(
                        qhatw[:, t * 512:t * 512 + 512], sqS[:, t * 512:t * 512 + 512],
                        onorm_s[:, 0:1], None, ALU.mult), inc="dv")
                DMA("gpsimd", ogated[h * DV:(h + 1) * DV, t0:t0 + BLK], qhatw[:, :],
                    waits=[("dv", p.val("dv"))])
                p.emit("vector", lambda e: e.engine_nop(), waits=[("ld", p.val("ld"))], inc="dv")

        p.emit("gpsimd", lambda e: e.engine_nop(), waits=[("st", p.val("st"))], inc="po")
        _run_plan(nc, p, sems)
    es.close()
    return nc


def build_post():
    nc = bass.Bass("TRN2", target_bir_lowering=False)
    dt = nc.dram_tensor
    ofull = dt("ofull", [D, TQ], F32, kind="ExternalInput").ap()
    xTs = dt("xTs", [D, TQ], F32, kind="ExternalInput").ap()
    woT = dt("woT", [D, D], F32, kind="ExternalInput").ap()
    wg8 = dt("wg8", [D, 4 * D], I8, kind="ExternalInput").ap()
    wu8 = dt("wu8", [D, 4 * D], I8, kind="ExternalInput").ap()
    wd8 = dt("wd8", [4 * D, D], I8, kind="ExternalInput").ap()
    sgate = dt("sgate", [4 * D, 1], F32, kind="ExternalInput").ap()
    sup = dt("sup", [4 * D, 1], F32, kind="ExternalInput").ap()
    sdown = dt("sdown", [D, 1], F32, kind="ExternalInput").ap()
    onesr = dt("onesr", [1, 128], F32, kind="ExternalInput").ap()
    onescl = dt("onescl", [128, 1], F32, kind="ExternalInput").ap()
    outT = dt("outT", [D, TQ], F32, kind="ExternalOutput").ap()
    dbgT = (dt("dbgT", [D, TQ], F32, kind="ExternalOutput").ap() if DBG_B else None)
    x1d = dt("x1d", [D, TQ], BF16).ap()

    es = contextlib.ExitStack()
    osb = es.enter_context(nc.sbuf_tensor("osb", [128, 16 * TQ], BF16))
    x1p = es.enter_context(nc.sbuf_tensor("x1p", [128, 16 * TQ], BF16))
    woslab0 = es.enter_context(nc.sbuf_tensor("woslab0", [128, 16 * 128], BF16))
    xtile = es.enter_context(nc.sbuf_tensor("xtile", [128, 512], F32))
    sq2 = es.enter_context(nc.sbuf_tensor("sq2", [128, 2 * 512], BF16))
    x1sl = es.enter_context(nc.sbuf_tensor("x1sl", [128, 16 * 512], BF16))
    wgsl = es.enter_context(nc.sbuf_tensor("wgsl", [128, 2 * 16 * 256], BF16))
    wusl = es.enter_context(nc.sbuf_tensor("wusl", [128, 16 * 256], BF16))
    wdsl0 = es.enter_context(nc.sbuf_tensor("wdsl0", [128, 32 * 128], BF16))
    wdsl1 = es.enter_context(nc.sbuf_tensor("wdsl1", [128, 32 * 128], BF16))
    gsb = es.enter_context(nc.sbuf_tensor("gsb", [128, 512], BF16))
    usb = es.enter_context(nc.sbuf_tensor("usb", [128, 512], BF16))
    sgs = es.enter_context(nc.sbuf_tensor("sgs", [128, 64], F32))
    sus = es.enter_context(nc.sbuf_tensor("sus", [128, 64], F32))
    sds = es.enter_context(nc.sbuf_tensor("sds", [128, 16], F32))
    rowm = es.enter_context(nc.sbuf_tensor("rowm", [1, 512], F32))
    rowv = es.enter_context(nc.sbuf_tensor("rowv", [1, 512], F32))
    rowr = es.enter_context(nc.sbuf_tensor("rowr", [1, 512], F32))
    rowmb = es.enter_context(nc.sbuf_tensor("rowmb", [1, 512], BF16))
    rowrb = es.enter_context(nc.sbuf_tensor("rowrb", [1, 512], BF16))
    ones1b = es.enter_context(nc.sbuf_tensor("ones1b", [1, 128], BF16))
    onescb = es.enter_context(nc.sbuf_tensor("onescb", [128, 1], BF16))
    epsc = es.enter_context(nc.sbuf_tensor("epsc", [1, 1], F32))
    qb0 = es.enter_context(nc.psum_tensor("qb0", [128, 512], F32))
    qb1 = es.enter_context(nc.psum_tensor("qb1", [128, 512], F32))
    qbs = es.enter_context(nc.psum_tensor("qbs", [128, 512], F32))
    qbs2 = es.enter_context(nc.psum_tensor("qbs2", [128, 512], F32))
    qbc = es.enter_context(nc.psum_tensor("qbc", [128, 512], F32))
    qbc2 = es.enter_context(nc.psum_tensor("qbc2", [128, 512], F32))
    s_ld = es.enter_context(nc.semaphore("ld"))
    s_pe = es.enter_context(nc.semaphore("pe"))
    s_dv = es.enter_context(nc.semaphore("dv"))
    s_ac = es.enter_context(nc.semaphore("ac"))
    s_po = es.enter_context(nc.semaphore("po"))
    s_st = es.enter_context(nc.semaphore("st"))
    if True:
        sems = {"ld": s_ld, "pe": s_pe, "dv": s_dv, "ac": s_ac, "po": s_po, "st": s_st}
        p = Plan()

        def DMA(eng, out, in_, waits=(), slow=False):
            return p.emit(eng, lambda e, o=out, i=in_, sl=slow: e.dma_start(
                out=o, in_=i, allow_slow_non_contiguous=sl),
                waits=waits, inc="ld", amt=16)

        def ST(out, in_, waits=()):
            return p.emit("sync", lambda e, o=out, i=in_: e.dma_start(out=o, in_=i),
                          waits=waits, inc="st", amt=16)

        DMA("gpsimd", ones1b[:, :], onesr[:, :])
        DMA("gpsimd", onescb[:, :], onescl[:, :])
        DMA("gpsimd", sgs[:, :].unsqueeze(2), sgate[:, :].rearrange("(m p) one -> p m one", p=128), slow=True)
        DMA("gpsimd", sus[:, :].unsqueeze(2), sup[:, :].rearrange("(m p) one -> p m one", p=128), slow=True)
        DMA("gpsimd", sds[:, :].unsqueeze(2), sdown[:, :].rearrange("(m p) one -> p m one", p=128), slow=True)
        DMA("gpsimd", osb[:, :].rearrange("p (kc t) -> p kc t", kc=16),
            ofull[:, :].rearrange("(kc kp) t -> kp kc t", kp=128))
        p.emit("vector", lambda e: e.memset(epsc[:, :], LN_EPS), inc="dv")
        ld0 = p.val("ld")

        ps_free = {0: ("dv", 0), 1: ("dv", 0)}
        xt_free = ("dv", 0)
        for m in range(16):
            DMA("gpsimd", woslab0[:, :].rearrange("p (kc mm) -> p kc mm", kc=16),
                woT[:, m * 128:(m + 1) * 128].rearrange("(kc kp) mm -> kp kc mm", kp=128),
                waits=[("pe", p.val("pe"))])
            ld_slab = p.val("ld")
            for tt in range(4):
                ps = (qb0, qb1)[tt % 2]
                DMA("gpsimd", xtile[:, :], xTs[m * 128:(m + 1) * 128, tt * 512:(tt + 1) * 512],
                    waits=[xt_free] + (p.bar() if tt == 0 else []))
                ldx = p.val("ld")
                for kc in range(16):
                    p.emit("tensor", lambda e, o=ps[0:128, 0:512],
                           w=woslab0[:, kc * 128:(kc + 1) * 128],
                           x=osb[:, kc * TQ + tt * 512: kc * TQ + tt * 512 + 512],
                           a=(kc == 0), b=(kc == 15): e.matmul(o, w, x, start=a, stop=b),
                           waits=([("ld", max(ld0, ld_slab)), ps_free[tt % 2]] +
                                  (p.bar() if tt == 0 else [])) if kc == 0 else (),
                           inc="pe" if kc == 15 else None)
                p.emit("vector", lambda e, d=x1p[:, m * TQ + tt * 512: m * TQ + tt * 512 + 512],
                       s=ps: e.tensor_tensor(d, s[0:128, 0:512], xtile[:, :], ALU.add),
                       waits=[("pe", p.val("pe")), ("ld", ldx)], inc="dv")
                ps_free[tt % 2] = ("dv", p.val("dv"))
                xt_free = ("dv", p.val("dv"))
        phase_done = [("dv", p.val("dv"))]
        if DBG_B == 1:
            DMA("gpsimd", dbgT[:, :].rearrange("(kc kp) t -> kp kc t", kp=128),
                x1p[:, :].rearrange("p (kc t) -> p kc t", kc=16), waits=phase_done)

        def layernorm(xbuf, store_to=None):
            init = list(phase_done)
            for tt in range(4):
                for kc in range(16):
                    p.emit("tensor", lambda e, k=kc, t=tt: e.matmul(
                        qbs[0:1, 0:512], onescb[:, :],
                        xbuf[:, k * TQ + t * 512: k * TQ + t * 512 + 512],
                        start=(k == 0), stop=(k == 15)),
                        waits=(init + [("ld", ld0)] + p.bar()) if kc == 0 else (),
                        inc="pe" if kc == 15 else None)
                sum_done = ("pe", p.val("pe"))
                mm_done = {0: ("pe", 0), 1: ("pe", 0)}
                for kc in range(16):
                    buf = kc % 2
                    p.emit("scalar", lambda e, k=kc, t=tt, b2=buf: e.activation(
                        sq2[:, b2 * 512:b2 * 512 + 512],
                        xbuf[:, k * TQ + t * 512: k * TQ + t * 512 + 512], AF.Square),
                        waits=list(init) + [mm_done[buf]], inc="ac")
                    p.emit("tensor", lambda e, k=kc, b2=buf: e.matmul(
                        qbs2[0:1, 0:512], onescb[:, :], sq2[:, b2 * 512:b2 * 512 + 512],
                        start=(k == 0), stop=(k == 15)),
                        waits=[("ac", p.val("ac"))] + ([("dv", p.val("dv"))] if kc == 0 else []),
                        inc="pe")
                    mm_done[buf] = ("pe", p.val("pe"))
                sq_done = ("pe", p.val("pe"))
                sl = slice(0, 512)
                p.emit("vector", lambda e, s=sl: e.tensor_scalar(rowm[:, s], qbs[0:1, 0:512],
                                                                 1.0 / D, None, ALU.mult),
                       waits=[sum_done], inc="dv")
                p.emit("vector", lambda e, s=sl: e.tensor_scalar(rowv[:, s], qbs2[0:1, 0:512],
                                                                 1.0 / D, None, ALU.mult),
                       waits=[sq_done], inc="dv")
                p.emit("vector", lambda e, s=sl: e.tensor_tensor(rowr[:, s], rowm[:, s],
                                                                 rowm[:, s], ALU.mult), inc="dv")
                p.emit("vector", lambda e, s=sl: e.tensor_tensor(rowv[:, s], rowv[:, s],
                                                                 rowr[:, s], ALU.subtract), inc="dv")
                p.emit("scalar", lambda e, s=sl: e.activation(rowr[:, s], rowv[:, s], AF.Ln,
                                                              bias=epsc[:, 0:1], scale=1.0),
                       waits=[("dv", p.val("dv"))], inc="ac")
                p.emit("scalar", lambda e, s=sl: e.activation(rowr[:, s], rowr[:, s], AF.Exp,
                                                              scale=-0.5), inc="ac")
                p.emit("vector", lambda e, s=sl: e.tensor_copy(rowrb[:, s], rowr[:, s]),
                       waits=[("ac", p.val("ac"))], inc="dv")
                p.emit("vector", lambda e, s=sl: e.tensor_tensor(rowm[:, s], rowm[:, s],
                                                                 rowr[:, s], ALU.mult), inc="dv")
                p.emit("vector", lambda e, s=sl: e.tensor_copy(rowmb[:, s], rowm[:, s]), inc="dv")
                rb_done = ("dv", p.val("dv"))
                p.emit("tensor", lambda e, s=sl: e.matmul(qbc[0:128, 0:512], ones1b[:, :],
                                                          rowrb[:, s], start=True, stop=True),
                       waits=[rb_done], inc="pe")
                p.emit("tensor", lambda e, s=sl: e.matmul(qbc2[0:128, 0:512], ones1b[:, :],
                                                          rowmb[:, s], start=True, stop=True),
                       inc="pe")
                bc_done = ("pe", p.val("pe"))
                for kc in range(16):
                    xa = xbuf[:, kc * TQ + tt * 512: kc * TQ + tt * 512 + 512]
                    if store_to is not None:
                        p.emit("vector", lambda e, x=xa: e.tensor_tensor(xtile[:, :], x,
                                                                         qbc[0:128, 0:512],
                                                                         ALU.mult),
                               waits=[bc_done, ("st", p.val("st"))], inc="dv")
                        p.emit("vector", lambda e: e.tensor_tensor(xtile[:, :], xtile[:, :],
                                                                   qbc2[0:128, 0:512],
                                                                   ALU.subtract), inc="dv")
                        ST(store_to[kc * 128:(kc + 1) * 128, tt * 512:(tt + 1) * 512], xtile[:, :],
                           waits=[("dv", p.val("dv"))])
                    else:
                        p.emit("vector", lambda e, x=xa: e.tensor_tensor(x, x, qbc[0:128, 0:512],
                                                                         ALU.mult),
                               waits=[bc_done], inc="dv")
                        p.emit("vector", lambda e, x=xa: e.tensor_tensor(x, x, qbc2[0:128, 0:512],
                                                                         ALU.subtract), inc="dv")
            return ("dv", p.val("dv"))

        ln1_done = layernorm(x1p)
        if DBG_B == 2:
            DMA("gpsimd", dbgT[:, :].rearrange("(kc kp) t -> kp kc t", kp=128),
                x1p[:, :].rearrange("p (kc t) -> p kc t", kc=16), waits=[ln1_done])
        ST(x1d[:, :].rearrange("(kc kp) t -> kp kc t", kp=128), x1p[:, :].rearrange("p (kc t) -> p kc t", kc=16), waits=[ln1_done])
        x1_spill = ("st", p.val("st"))

        ev_g = ("ac", 0)
        ev_u = ("dv", 0)
        ev_fuse = ("dv", 0)
        ev_d = ("dv", 0)
        x1sl_free = ("pe", 0)
        wg_free = {0: ("pe", 0), 1: ("pe", 0)}
        wu_free = ("pe", 0)
        wd_free = {0: ("pe", 0), 1: ("pe", 0)}
        for tq in range(4):
            DMA("gpsimd", x1sl[:, :].rearrange("p (kc t) -> p kc t", kc=16),
                x1d[:, tq * 512:(tq + 1) * 512].rearrange("(kc kp) t -> kp kc t", kp=128),
                waits=[x1_spill, x1sl_free, ev_d] + p.bar())
            ldx1 = p.val("ld")
            for ib in range(32):
                par = ib % 2
                gb = par * 4096
                DMA("gpsimd", wgsl[:, gb:gb + 4096].rearrange("p (kc mm) -> p kc mm", kc=16),
                    wg8[:, ib * 256:(ib + 1) * 256].rearrange("(kc kp) mm -> kp kc mm", kp=128),
                    waits=[wg_free[par]] + (p.bar() if ib == 0 else []))
                DMA("gpsimd", wusl[:, :].rearrange("p (kc mm) -> p kc mm", kc=16),
                    wu8[:, ib * 256:(ib + 1) * 256].rearrange("(kc kp) mm -> kp kc mm", kp=128),
                    waits=[wu_free])
                ldw = p.val("ld")
                for io in range(2):
                    ii = ib * 2 + io
                    for kc in range(16):
                        p.emit("tensor", lambda e, k=kc, o=io, g2=gb: e.matmul(
                            qb0[0:128, 0:512],
                            wgsl[:, g2 + k * 256 + o * 128: g2 + k * 256 + o * 128 + 128],
                            x1sl[:, k * 512:(k + 1) * 512], start=(k == 0), stop=(k == 15)),
                            waits=([("ld", max(ldx1, ldw)), ev_g] +
                                   (p.bar() if (io == 0 and ib == 0) else [])) if kc == 0 else (),
                            inc="pe" if kc == 15 else None)
                    if io == 1:
                        wg_free[par] = ("pe", p.val("pe"))
                    p.emit("scalar", lambda e, i2=ii: e.activation(gsb[:, :], qb0[0:128, 0:512],
                                                                   AF.Silu,
                                                                   scale=sgs[:, i2:i2 + 1]),
                           waits=[("pe", p.val("pe")), ev_fuse], inc="ac")
                    ev_g = ("ac", p.val("ac"))
                    for kc in range(16):
                        p.emit("tensor", lambda e, k=kc, o=io: e.matmul(
                            qb1[0:128, 0:512],
                            wusl[:, k * 256 + o * 128: k * 256 + o * 128 + 128],
                            x1sl[:, k * 512:(k + 1) * 512], start=(k == 0), stop=(k == 15)),
                            waits=[ev_u] if kc == 0 else (), inc="pe" if kc == 15 else None)
                    if io == 1:
                        wu_free = ("pe", p.val("pe"))
                    p.emit("vector", lambda e, i2=ii: e.tensor_scalar(usb[:, :], qb1[0:128, 0:512],
                                                                      sus[:, i2:i2 + 1], None,
                                                                      ALU.mult),
                           waits=[("pe", p.val("pe")), ev_fuse], inc="dv")
                    ev_u = ("dv", p.val("dv"))
                    p.emit("vector", lambda e, i2=ii: e.tensor_tensor(
                        osb[:, i2 * 512:(i2 + 1) * 512], gsb[:, :], usb[:, :], ALU.mult),
                        waits=[ev_g], inc="dv")
                    ev_fuse = ("dv", p.val("dv"))
            fuse_done = ("dv", p.val("dv"))
            if DBG_B == 4 and tq == 0:
                DMA("gpsimd", dbgT[:, :].rearrange("(kc kp) t -> kp kc t", kp=128),
                    osb[:, :].rearrange("p (kc t) -> p kc t", kc=16), waits=[fuse_done])
            for m in range(16):
                for half in range(2):
                    wdb = wdsl0 if half == 0 else wdsl1
                    DMA("gpsimd", wdb[:, 0:32 * 128].rearrange("p (kc mm) -> p kc mm", kc=32),
                        wd8[4096 * half:4096 * (half + 1), m * 128:(m + 1) * 128].rearrange(
                            "(kc kp) mm -> kp kc mm", kp=128),
                        waits=[wd_free[half]] + (p.bar() if (tq == 0 and m == 0 and half == 0)
                                                 else []))
                    ldwd = p.val("ld")
                    for kc in range(32):
                        kk = half * 32 + kc
                        p.emit("tensor", lambda e, k=kc, k2=kk, wb=wdb: e.matmul(
                            qbs[0:128, 0:512], wb[:, k * 128:(k + 1) * 128],
                            osb[:, k2 * 512:(k2 + 1) * 512], start=(k2 == 0), stop=(k2 == 63)),
                            waits=[fuse_done, ("ld", ldwd), ev_d] if kc == 0 else (),
                            inc="pe" if kc == 31 else None)
                    wd_free[half] = ("pe", p.val("pe"))
                p.emit("vector", lambda e, mm=m, t=tq: e.scalar_tensor_tensor(
                    x1p[:, mm * TQ + t * 512: mm * TQ + t * 512 + 512],
                    qbs[0:128, 0:512], sds[:, mm:mm + 1],
                    x1sl[:, mm * 512:(mm + 1) * 512], ALU.mult, ALU.add),
                    waits=[("pe", p.val("pe")), ("st", p.val("st"))], inc="dv")
                ev_d = ("dv", p.val("dv"))
            x1sl_free = ("pe", p.val("pe"))
        phase_done = [ev_d]
        if DBG_B == 3:
            DMA("gpsimd", dbgT[:, :].rearrange("(kc kp) t -> kp kc t", kp=128),
                x1p[:, :].rearrange("p (kc t) -> p kc t", kc=16), waits=[ev_d])
        layernorm(x1p, store_to=outT)
        p.emit("gpsimd", lambda e: e.engine_nop(), waits=[("st", p.val("st"))], inc="po")
        _run_plan(nc, p, sems)
    es.close()
    return nc


def _consts():
    tt = np.arange(C)
    ii = np.arange(R)
    p_t = np.repeat(tt[None, :], R, 0).reshape(-1)
    p_i = np.repeat(ii[:, None], C, 1).reshape(-1)
    maskM = ((p_t[:, None] > p_t[None, :]) |
             ((p_t[:, None] == p_t[None, :]) & (p_i[:, None] > p_i[None, :]))).astype(np.float32)
    maskAT = (np.arange(C)[None, :] >= p_t[:, None]).astype(np.float32)
    return {
        "identc": np.eye(128, dtype=np.float32),
        "maskMc": maskM,
        "maskMTc": np.ascontiguousarray(maskM.T),
        "maskATc": maskAT,
        "onesr": np.ones((1, 128), np.float32),
        "onescl": np.ones((128, 1), np.float32),
    }


def _mixer_in_maps(inp, cst, xT):
    in_maps = []
    for c in range(8):
        b, hq = divmod(c, 4)
        hs = slice(4 * hq * DK, 4 * (hq + 1) * DK)
        hks = slice(4 * hq * R * DK, 4 * (hq + 1) * R * DK)
        hbs = slice(4 * hq * R, 4 * (hq + 1) * R)
        m = {
            "xT": xT[b],
            "wqT": np.ascontiguousarray(inp["L0_attn_Wq"][hs].T),
            "wkT": np.ascontiguousarray(inp["L0_attn_Wk"][hks].T),
            "wvT": np.ascontiguousarray(inp["L0_attn_Wv"][hks].T),
            "wbT": np.ascontiguousarray(inp["L0_attn_Wb"][hbs].T),
            "wf1T": np.ascontiguousarray(inp["L0_attn_Wf1"].T),
            "wg1T": np.ascontiguousarray(inp["L0_attn_Wg1"].T),
            "wf2T": np.ascontiguousarray(inp["L0_attn_Wf2"][hs].T),
            "wg2T": np.ascontiguousarray(inp["L0_attn_Wg2"][hs].T),
            "convq": np.ascontiguousarray(inp["L0_attn_conv_q"][hs]),
            "convk": np.ascontiguousarray(inp["L0_attn_conv_k"][hks]),
            "convv": np.ascontiguousarray(inp["L0_attn_conv_v"][hks]),
            "dtb": np.ascontiguousarray(inp["L0_attn_dt_bias"][hs, None]),
            "alogr": np.ascontiguousarray(inp["L0_attn_A_log"][None, 4 * hq:4 * hq + 4]),
            "bg1": np.ascontiguousarray(inp["L0_attn_bg1"][:, None]),
            "bg2": np.ascontiguousarray(inp["L0_attn_bg2"][hs, None]),
            "onormw": np.ascontiguousarray(inp["L0_attn_onorm_w"][:, None]),
        }
        m.update(cst)
        in_maps.append(m)
    return in_maps


def kernel(**inp):
    cst = _consts()
    x = np.ascontiguousarray(inp["L0_attn_hidden_states"])
    xT = [np.ascontiguousarray(x[b].T) for b in range(2)]

    nc_a = build_mixer()
    res_a = run_bass_kernel_spmd(nc_a, _mixer_in_maps(inp, cst, xT), core_ids=list(range(8)))
    og = [np.array(res_a.results[c]["ogated"], dtype=np.float32, copy=True) for c in range(8)]

    in_maps_b = []
    for c in range(8):
        b, tq = divmod(c, 4)
        ts = slice(tq * TQ, (tq + 1) * TQ)
        ofull = np.concatenate([og[4 * b + j][:, ts] for j in range(4)], axis=0)
        in_maps_b.append({
            "ofull": np.ascontiguousarray(ofull),
            "xTs": np.ascontiguousarray(xT[b][:, ts]),
            "woT": np.ascontiguousarray(inp["L0_attn_Wo"].T),
            "wg8": np.ascontiguousarray(inp["L0_ffn_w_gate"]),
            "wu8": np.ascontiguousarray(inp["L0_ffn_w_up"]),
            "wd8": np.ascontiguousarray(inp["L0_ffn_w_down"]),
            "sgate": np.ascontiguousarray(inp["L0_ffn_s_gate"][:, None]),
            "sup": np.ascontiguousarray(inp["L0_ffn_s_up"][:, None]),
            "sdown": np.ascontiguousarray(inp["L0_ffn_s_down"][:, None]),
            "onesr": cst["onesr"],
            "onescl": cst["onescl"],
        })

    nc_b = build_post()
    res_b = run_bass_kernel_spmd(nc_b, in_maps_b, core_ids=list(range(8)))

    out = np.empty((2, T, D), np.float32)
    for c in range(8):
        b, tq = divmod(c, 4)
        out[b, tq * TQ:(tq + 1) * TQ, :] = np.array(res_b.results[c]["outT"], copy=True).T
    return out
